# version 16
# baseline (speedup 1.0000x reference)
import math
import numpy as np
import ml_dtypes
import concourse.bass as bass
import concourse.mybir as mybir
from concourse.bass_utils import run_bass_kernel_spmd

F32 = mybir.dt.float32
BF16 = mybir.dt.bfloat16
I32 = mybir.dt.int32
U8 = mybir.dt.uint8
AF = mybir.ActivationFunctionType
ALU = mybir.AluOpType
AX = mybir.AxisListType

B, T, H = 4, 2048, 2048
HC = 16
NOWN = 1024
QR, QRC = 1024, 8
KVR, KVC = 512, 4
NH = 16
NSLOT = 8
NIB = 64
SCALE = float(1.0 / np.sqrt(np.float64(192.0)))
EPS_LN, EPS_RMS = 1e-5, 1e-6
NEGBIG = -1.0e9


class Bld:
    def __init__(self, nc):
        self.nc = nc
        self.sems = {}
        self.cnt = {}

    def sem(self, name):
        if name not in self.sems:
            self.sems[name] = self.nc.alloc_semaphore(name)
            self.cnt[name] = 0
        return self.sems[name]

    def inc(self, instr, name, k=1):
        instr.then_inc(self.sem(name), k)
        self.cnt[name] += k
        return (name, self.cnt[name])

    def tok(self, name):
        return (name, self.cnt.get(name, 0))

    def wait(self, eng, tok):
        if tok is None:
            return
        if isinstance(tok, list):
            for t in tok:
                self.wait(eng, t)
            return
        eng.wait_ge(self.sem(tok[0]), tok[1])


def build_program(taps=(), detect_races=False):
    nc = bass.Bass("TRN2", target_bir_lowering=False,
                   detect_race_conditions=detect_races)
    TEN, VEC, SCA, GP, SY = nc.tensor, nc.vector, nc.scalar, nc.gpsimd, nc.sync
    bld = Bld(nc)
    ET = mybir.EngineType

    def cbar():
        nc.multi_engine_barrier([ET.PE, ET.Activation, ET.DVE])

    def dma_fence():
        for s in ("tV", "tC", "tT"):
            bld.wait(GP, bld.tok(s))
            bld.wait(SY, bld.tok(s))

    def dma_g(slot, dst, src, free_tok=None):
        bld.wait(GP, free_tok)
        return bld.inc(GP.dma_start(dst, src), "g_" + slot, 16)

    def dma_s(slot, dst, src, free_tok=None):
        bld.wait(SY, free_tok)
        return bld.inc(SY.dma_start(dst, src), "s_" + slot, 16)

    def din(name, shape, dtype):
        return nc.dram_tensor(name, shape, dtype, kind="ExternalInput").ap()

    xT = din("xT", [H, T], F32)
    xownT = din("xownT", [H, NOWN], F32)
    ccqD = din("ccq", [128, NOWN], F32)
    ssqD = din("ssq", [128, NOWN], F32)
    cckD = din("cck", [64, T], F32)
    sskD = din("ssk", [64, T], F32)
    smaskD = din("smask", [128, 256], F32)
    identD = din("ident", [128, 128], BF16)
    qsD = din("qsT", [128, QRC], F32)
    kvsD = din("kvsT", [128, KVC], F32)
    gsD = din("gsT", [128, NIB], F32)
    usD = din("usT", [128, NIB], F32)
    dsD = din("dsT", [128, HC], F32)
    w_cq = din("w_cq", [H, QR], F32)
    w_ckv = din("w_ckv", [H, KVR], F32)
    wkrAD = din("wkrA", [128, HC, 64], F32)
    wkrBD = din("wkrB", [128, HC, 64], F32)
    wqnD = din("wqn", [NH, 128, QRC, 128], F32)
    wqrAD = din("wqrA", [NH // 2, 128, QRC, 128], F32)
    wqrBD = din("wqrB", [NH // 2, 128, QRC, 128], F32)
    wdkD = din("wdk", [NH, 128, KVC, 128], F32)
    wdvD = din("wdv", [NH, 128, KVC, 128], F32)
    woD = din("wo", [HC, 128, NH, 128], F32)
    gwD = din("gw", [NIB, 128, HC, 128], U8)
    uwD = din("uw", [NIB, 128, HC, 128], U8)
    dwD = din("dw", [HC, 128, NIB, 128], U8)
    outT = nc.dram_tensor("outT", [H, NOWN], F32, kind="ExternalOutput").ap()

    def tap(name, sb_ap, shape):
        if name not in taps:
            return
        d = nc.dram_tensor("tap_" + name, list(shape), F32, kind="ExternalOutput").ap()
        cbar()
        for s in ("tV", "tC", "tT"):
            bld.wait(GP, bld.tok(s))
        t = bld.inc(GP.dma_start(d, sb_ap), "g_tap", 16)
        bld.wait(GP, t)

    from contextlib import ExitStack

    with ExitStack() as top:
        E = top.enter_context

        ident_sb = E(nc.sbuf_tensor("identS", [128, 128], BF16))
        smask_sb = E(nc.sbuf_tensor("smaskS", [128, 256], BF16))
        qs_sb = E(nc.sbuf_tensor("qs", [128, QRC], F32))
        kvs_sb = E(nc.sbuf_tensor("kvs", [128, KVC], F32))
        gs_sb = E(nc.sbuf_tensor("gs", [128, NIB], F32))
        us_sb = E(nc.sbuf_tensor("us", [128, NIB], F32))
        ds_sb = E(nc.sbuf_tensor("ds", [128, HC], F32))
        ones_b = E(nc.sbuf_tensor("ones_b", [128, 1], BF16))
        nones_b = E(nc.sbuf_tensor("nones_b", [128, 1], BF16))
        ones_fr = E(nc.sbuf_tensor("ones_fr", [1, 128], F32))
        eps_ln = E(nc.sbuf_tensor("eps_ln", [1, 1], F32))
        eps_rms = E(nc.sbuf_tensor("eps_rms", [1, 1], F32))
        n128 = E(nc.sbuf_tensor("n128", [128, 1], F32))

        VEC.memset(n128[:], -128.0)
        VEC.memset(ones_b[:], 1.0)
        VEC.memset(nones_b[:], -1.0)
        VEC.memset(ones_fr[:], 1.0)
        VEC.memset(eps_ln[:], EPS_LN)
        VEC.memset(eps_rms[:], EPS_RMS)
        dma_s("const", ident_sb[:], identD)
        t_smask = dma_g("constg", smask_sb[:], smaskD)
        dma_s("const", qs_sb[:], qsD)
        dma_s("const", kvs_sb[:], kvsD)
        dma_s("const", gs_sb[:], gsD)
        dma_s("const", us_sb[:], usD)
        t_const = dma_s("const", ds_sb[:], dsD)

        x1_sb = E(nc.sbuf_tensor("x1", [128, HC * 1024], F32, side="right"))

        with ExitStack() as lvA:
            A_ = lvA.enter_context
            lvP2 = lvA.enter_context(ExitStack())
            P2_ = lvP2.enter_context
            normq_sb = P2_(nc.sbuf_tensor("normq", [128, QRC * 1024], BF16))
            normkv_sb = P2_(nc.sbuf_tensor("normkv", [128, KVC * T], BF16))
            krope_sb = P2_(nc.sbuf_tensor("krope", [64, T], BF16))
            ccq_sb = P2_(nc.sbuf_tensor("ccqS", [128, NOWN], BF16))
            ssq_sb = P2_(nc.sbuf_tensor("ssqS", [128, NOWN], BF16))
            lvW = lvA.enter_context(ExitStack())
            W_ = lvW.enter_context
            m_bf = W_(nc.sbuf_tensor("m_bf", [1, T], BF16))
            r_b = W_(nc.sbuf_tensor("r_b", [128, T], F32))
            mo_bf = W_(nc.sbuf_tensor("mo_bf", [1, NOWN], BF16))
            ro_b = W_(nc.sbuf_tensor("ro_b", [128, NOWN], F32))
            ncs_q = W_(nc.sbuf_tensor("ncs_q", [1, QR], BF16))
            ncs_kv = W_(nc.sbuf_tensor("ncs_kv", [1, KVR], BF16))
            ncs_kA = W_(nc.sbuf_tensor("ncs_kA", [1, 64], BF16))
            ncs_kB = W_(nc.sbuf_tensor("ncs_kB", [1, 64], BF16))
            wkrA_sb = W_(nc.sbuf_tensor("wkrAS", [128, HC * 64], BF16))
            wkrB_sb = W_(nc.sbuf_tensor("wkrBS", [128, HC * 64], BF16))

            dma_g("tabq", ccq_sb[:], ccqD)
            t_ccq = dma_g("tabq", ssq_sb[:], ssqD)
            t_wkrA = dma_g("wkrA", wkrA_sb[:], wkrAD)
            t_wkrB = dma_g("wkrB", wkrB_sb[:], wkrBD)

            def ln_stats(src_ap, width, m_out, r_out, pref):
                NR = 3
                cw = min(width, 1024)
                with ExitStack() as st_:
                    S_ = st_.enter_context
                    xin = [S_(nc.sbuf_tensor(f"{pref}xin{i}", [128, cw], F32)) for i in range(NR)]
                    xbf = [S_(nc.sbuf_tensor(f"{pref}xbf{i}", [128, cw], BF16)) for i in range(NR)]
                    xsq = [S_(nc.sbuf_tensor(f"{pref}xsq{i}", [128, cw], BF16)) for i in range(NR)]
                    rows = S_(nc.sbuf_tensor(f"{pref}rows", [1, 2 * width], F32))
                    steps = [(hc, c0) for hc in range(HC) for c0 in range(0, width, cw)]
                    with ExitStack() as ps_:
                        psx = ps_.enter_context(nc.psum_tensor(f"{pref}psx", [1, width], F32))
                        psq = ps_.enter_context(nc.psum_tensor(f"{pref}psq", [1, width], F32))
                        free_in = [None] * NR
                        free_mm = [None] * NR
                        for i, (hc, c0) in enumerate(steps):
                            sl = i % NR
                            t_x = dma_s(f"{pref}x{sl}", xin[sl][:],
                                        src_ap[hc * 128:(hc + 1) * 128, c0:c0 + cw],
                                        free_tok=free_in[sl])
                            bld.wait(VEC, t_x)
                            bld.wait(VEC, free_mm[sl])
                            tv = bld.inc(VEC.tensor_copy(xbf[sl][:], xin[sl][:]), "tV")
                            bld.wait(SCA, t_x)
                            bld.wait(SCA, free_mm[sl])
                            tc = bld.inc(SCA.activation(xsq[sl][:], xin[sl][:], AF.Square), "tC")
                            free_in[sl] = [tv, tc]
                            bld.wait(TEN, tv)
                            bld.wait(TEN, tc)
                            last = None
                            for tt in range(cw // 512):
                                TEN.matmul(psx[0:1, c0 + tt * 512:c0 + (tt + 1) * 512], ones_b[:],
                                           xbf[sl][:, tt * 512:(tt + 1) * 512],
                                           start=(hc == 0), stop=(hc == HC - 1))
                                last = TEN.matmul(psq[0:1, c0 + tt * 512:c0 + (tt + 1) * 512],
                                                  ones_b[:], xsq[sl][:, tt * 512:(tt + 1) * 512],
                                                  start=(hc == 0), stop=(hc == HC - 1))
                            free_mm[sl] = bld.inc(last, "tT")
                        bld.wait(VEC, free_mm[0])
                        bld.wait(VEC, free_mm[1])
                        W = width
                        VEC.tensor_scalar(rows[0:1, 0:W], psx[0:1, :], 1.0 / H, None, ALU.mult)
                        VEC.tensor_scalar(rows[0:1, W:2 * W], psq[0:1, :], 1.0 / H, None, ALU.mult)
                        VEC.scalar_tensor_tensor(psx[0:1, :], rows[0:1, 0:W], -1.0,
                                                 rows[0:1, 0:W], ALU.mult, ALU.mult)
                        VEC.tensor_tensor(rows[0:1, W:2 * W], rows[0:1, W:2 * W],
                                          psx[0:1, :], ALU.add)
                        tv = bld.inc(VEC.tensor_copy(m_out[0:1, :], rows[0:1, 0:W]), "tV")
                        bld.wait(TEN, tv)
                    W = width
                    bld.wait(SCA, tv)
                    SCA.activation(rows[0:1, W:2 * W], rows[0:1, W:2 * W], AF.Ln, bias=eps_ln[:])
                    tc = bld.inc(SCA.activation(rows[0:1, W:2 * W], rows[0:1, W:2 * W],
                                                AF.Exp, scale=-0.5), "tC")
                    with nc.psum_tensor(f"{pref}psb", [128, width], F32) as psb:
                        bld.wait(TEN, tc)
                        last = None
                        for tt in range(width // 512):
                            last = TEN.matmul(psb[:, tt * 512:(tt + 1) * 512], ones_fr[:],
                                              rows[0:1, W + tt * 512: W + (tt + 1) * 512],
                                              start=True, stop=True)
                        tmm = bld.inc(last, "tT")
                        bld.wait(VEC, tmm)
                        tv = bld.inc(VEC.tensor_copy(r_out[:, :], psb[:, :]), "tV")
                        bld.wait(TEN, tv)

            ln_stats(xT, T, m_bf, r_b, "lnf")
            dma_fence()
            ln_stats(xownT, NOWN, mo_bf, ro_b, "lno")
            tap("m_bf", m_bf[:], (1, T))
            tap("r_b", r_b[:], (128, T))

            dma_fence()
            with ExitStack() as st_:
                S_ = st_.enter_context
                wbuf = [S_(nc.sbuf_tensor(f"csw{i}", [128, QR], BF16)) for i in range(2)]
                with ExitStack() as ps_:
                    pq = ps_.enter_context(nc.psum_tensor("pcsq", [1, QR], F32))
                    pkv = ps_.enter_context(nc.psum_tensor("pcskv", [1, KVR], F32))
                    pka = ps_.enter_context(nc.psum_tensor("pcska", [1, 64], F32))
                    pkb = ps_.enter_context(nc.psum_tensor("pcskb", [1, 64], F32))
                    free = [None, None]
                    for hc in range(HC):
                        sl = hc % 2
                        t_w = dma_g(f"csw{sl}", wbuf[sl][:], w_cq[hc * 128:(hc + 1) * 128, :],
                                    free_tok=free[sl])
                        bld.wait(TEN, t_w)
                        last = None
                        for tt in range(2):
                            last = TEN.matmul(pq[0:1, tt * 512:(tt + 1) * 512], nones_b[:],
                                              wbuf[sl][:, tt * 512:(tt + 1) * 512],
                                              start=(hc == 0), stop=(hc == HC - 1))
                        free[sl] = bld.inc(last, "tT")
                    for hc in range(HC):
                        sl = hc % 2
                        t_w = dma_g(f"csw{sl}", wbuf[sl][:, 0:KVR], w_ckv[hc * 128:(hc + 1) * 128, :],
                                    free_tok=free[sl])
                        bld.wait(TEN, t_w)
                        if hc == 0:
                            bld.wait(TEN, t_wkrA)
                            bld.wait(TEN, t_wkrB)
                        TEN.matmul(pkv[0:1, :], nones_b[:], wbuf[sl][:, 0:KVR],
                                   start=(hc == 0), stop=(hc == HC - 1))
                        TEN.matmul(pka[0:1, :], nones_b[:], wkrA_sb[:, hc * 64:(hc + 1) * 64],
                                   start=(hc == 0), stop=(hc == HC - 1))
                        last = TEN.matmul(pkb[0:1, :], nones_b[:], wkrB_sb[:, hc * 64:(hc + 1) * 64],
                                          start=(hc == 0), stop=(hc == HC - 1))
                        free[sl] = bld.inc(last, "tT")
                    bld.wait(VEC, free[1])
                    VEC.tensor_copy(ncs_q[:], pq[0:1, :])
                    VEC.tensor_copy(ncs_kv[:], pkv[0:1, :])
                    VEC.tensor_copy(ncs_kA[:], pka[0:1, :])
                    tv = bld.inc(VEC.tensor_copy(ncs_kB[:], pkb[0:1, :]), "tV")
                    bld.wait(TEN, tv)
            tap("ncs_q", ncs_q[:], (1, QR))

            dma_fence()
            with ExitStack() as st_:
                S_ = st_.enter_context
                xqc_sb = S_(nc.sbuf_tensor("xqc", [128, QRC * 1024], BF16))
                xin = [S_(nc.sbuf_tensor(f"qxin{i}", [128, 512], F32)) for i in range(4)]
                xbf = [S_(nc.sbuf_tensor(f"qxbf{i}", [128, 512], BF16)) for i in range(4)]
                wbuf = [S_(nc.sbuf_tensor(f"qw{i}", [128, QR], BF16)) for i in range(3)]
                sqb = [S_(nc.sbuf_tensor(f"qsq{i}", [128, 1024], BF16)) for i in range(2)]
                rowq = S_(nc.sbuf_tensor("rowq", [1, NOWN], F32))
                rq_b = S_(nc.sbuf_tensor("rq_b", [128, NOWN], F32))
                with ExitStack() as ps_:
                    pxq = ps_.enter_context(nc.psum_tensor("pxq", [128, QRC * 512], F32))
                    freew = [None] * 3
                    freex = [None] * 4
                    freeb = [None] * 4
                    for th in range(2):
                        for hc in range(HC):
                            i = th * HC + hc
                            sl = i % 4
                            wl = i % 3
                            t_x = dma_s(f"qx{sl}", xin[sl][:],
                                        xownT[hc * 128:(hc + 1) * 128, th * 512:(th + 1) * 512],
                                        free_tok=freex[sl])
                            t_w = dma_g(f"qw{wl}", wbuf[wl][:], w_cq[hc * 128:(hc + 1) * 128, :],
                                        free_tok=freew[wl])
                            bld.wait(VEC, t_x)
                            bld.wait(VEC, freeb[sl])
                            tv = bld.inc(VEC.tensor_copy(xbf[sl][:], xin[sl][:]), "tV")
                            freex[sl] = tv
                            bld.wait(TEN, tv)
                            bld.wait(TEN, t_w)
                            last = None
                            for qrc in range(QRC):
                                last = TEN.matmul(pxq[:, qrc * 512:(qrc + 1) * 512],
                                                  wbuf[wl][:, qrc * 128:(qrc + 1) * 128], xbf[sl][:],
                                                  start=(hc == 0), stop=False)
                            t_mm = bld.inc(last, "tT")
                            freew[wl] = t_mm
                            freeb[sl] = t_mm
                        last = None
                        for qrc in range(QRC):
                            last = TEN.matmul(pxq[:, qrc * 512:(qrc + 1) * 512],
                                              ncs_q[0:1, qrc * 128:(qrc + 1) * 128],
                                              mo_bf[0:1, th * 512:(th + 1) * 512],
                                              start=False, stop=True)
                        tmm = bld.inc(last, "tT")
                        bld.wait(VEC, tmm)
                        last = None
                        for qrc in range(QRC):
                            last = VEC.tensor_tensor(
                                xqc_sb[:, qrc * 1024 + th * 512: qrc * 1024 + (th + 1) * 512],
                                pxq[:, qrc * 512:(qrc + 1) * 512],
                                ro_b[:, th * 512:(th + 1) * 512], ALU.mult)
                        tv = bld.inc(last, "tV")
                        bld.wait(TEN, tv)
                with ExitStack() as ps_:
                    pss = ps_.enter_context(nc.psum_tensor("pssq", [1, NOWN], F32))
                    frees = [bld.tok("tV"), bld.tok("tV")]
                    for qrc in range(QRC):
                        sl = qrc % 2
                        bld.wait(SCA, frees[sl])
                        tc = bld.inc(SCA.activation(sqb[sl][:],
                                                    xqc_sb[:, qrc * 1024:(qrc + 1) * 1024],
                                                    AF.Square), "tC")
                        bld.wait(TEN, tc)
                        last = None
                        for tt in range(2):
                            last = TEN.matmul(pss[0:1, tt * 512:(tt + 1) * 512], ones_b[:],
                                              sqb[sl][:, tt * 512:(tt + 1) * 512],
                                              start=(qrc == 0), stop=(qrc == QRC - 1))
                        frees[sl] = bld.inc(last, "tT")
                    bld.wait(VEC, frees[(QRC - 1) % 2])
                    tv = bld.inc(VEC.tensor_scalar(rowq[0:1, 0:NOWN], pss[0:1, :],
                                                   1.0 / QR, None, ALU.mult), "tV")
                    bld.wait(SCA, tv)
                    SCA.activation(rowq[0:1, 0:NOWN], rowq[0:1, 0:NOWN], AF.Ln,
                                   bias=eps_rms[:])
                    tc = bld.inc(SCA.activation(rowq[0:1, 0:NOWN], rowq[0:1, 0:NOWN],
                                                AF.Exp, scale=-0.5), "tC")
                    bld.wait(TEN, tc)
                with nc.psum_tensor("psbq", [128, NOWN], F32) as psb:
                    last = None
                    for tt in range(2):
                        last = TEN.matmul(psb[:, tt * 512:(tt + 1) * 512], ones_fr[:],
                                          rowq[0:1, tt * 512:(tt + 1) * 512], start=True, stop=True)
                    tmm = bld.inc(last, "tT")
                    bld.wait(VEC, tmm)
                    bld.wait(VEC, t_const)
                    VEC.tensor_copy(rq_b[:], psb[:])
                    last = None
                    for qrc in range(QRC):
                        last = VEC.scalar_tensor_tensor(normq_sb[:, qrc * 1024:(qrc + 1) * 1024],
                                                        xqc_sb[:, qrc * 1024:(qrc + 1) * 1024],
                                                        qs_sb[:, qrc:qrc + 1], rq_b[:],
                                                        ALU.mult, ALU.mult)
                    tv = bld.inc(last, "tV")
                    bld.wait(TEN, tv)
            tap("norm_q", normq_sb[:], (128, QRC * 1024))

            dma_fence()
            with ExitStack() as st_:
                S_ = st_.enter_context
                cck_sb = S_(nc.sbuf_tensor("cckS", [64, T], BF16))
                ssk_sb = S_(nc.sbuf_tensor("sskS", [64, T], BF16))
                xkvc_sb = S_(nc.sbuf_tensor("xkvc", [128, KVC * T], BF16))
                xin = [S_(nc.sbuf_tensor(f"kxin{i}", [128, 512], F32)) for i in range(4)]
                xbf = [S_(nc.sbuf_tensor(f"kxbf{i}", [128, 512], BF16)) for i in range(4)]
                wbuf = [S_(nc.sbuf_tensor(f"kw{i}", [128, KVR], BF16)) for i in range(4)]
                tmpa = S_(nc.sbuf_tensor("ktmpa", [64, 512], F32))
                tmpb = S_(nc.sbuf_tensor("ktmpb", [64, 512], F32))
                sqb = [S_(nc.sbuf_tensor(f"ksq{i}", [128, 1024], BF16)) for i in range(2)]
                rowk = S_(nc.sbuf_tensor("rowk", [1, T], F32))
                rkv_b = S_(nc.sbuf_tensor("rkv_b", [128, T], F32))
                dma_g("tabk", cck_sb[:], cckD)
                t_tabk = dma_g("tabk", ssk_sb[:], sskD)
                with ExitStack() as ps_:
                    pkv = ps_.enter_context(nc.psum_tensor("pkv", [128, KVC * 512], F32))
                    pka = ps_.enter_context(nc.psum_tensor("pka", [64, 512], F32))
                    pkb = ps_.enter_context(nc.psum_tensor("pkb", [64, 512], F32))
                    freew = [None] * 4
                    freex = [None] * 4
                    freeb = [None] * 4
                    for tq in range(4):
                        for hc in range(HC):
                            i = tq * HC + hc
                            sl = i % 4
                            t_x = dma_s(f"kx{sl}", xin[sl][:],
                                        xT[hc * 128:(hc + 1) * 128, tq * 512:(tq + 1) * 512],
                                        free_tok=freex[sl])
                            t_w = dma_g(f"kw{sl}", wbuf[sl][:], w_ckv[hc * 128:(hc + 1) * 128, :],
                                        free_tok=freew[sl])
                            bld.wait(VEC, t_x)
                            bld.wait(VEC, freeb[sl])
                            tv = bld.inc(VEC.tensor_copy(xbf[sl][:], xin[sl][:]), "tV")
                            freex[sl] = tv
                            bld.wait(TEN, tv)
                            bld.wait(TEN, t_w)
                            for kvc in range(KVC):
                                TEN.matmul(pkv[:, kvc * 512:(kvc + 1) * 512],
                                           wbuf[sl][:, kvc * 128:(kvc + 1) * 128], xbf[sl][:],
                                           start=(hc == 0), stop=False)
                            TEN.matmul(pka[:, :], wkrA_sb[:, hc * 64:(hc + 1) * 64], xbf[sl][:],
                                       start=(hc == 0), stop=False)
                            last = TEN.matmul(pkb[:, :], wkrB_sb[:, hc * 64:(hc + 1) * 64],
                                              xbf[sl][:], start=(hc == 0), stop=False)
                            t_mm = bld.inc(last, "tT")
                            freew[sl] = t_mm
                            freeb[sl] = t_mm
                        mslice = m_bf[0:1, tq * 512:(tq + 1) * 512]
                        for kvc in range(KVC):
                            TEN.matmul(pkv[:, kvc * 512:(kvc + 1) * 512],
                                       ncs_kv[0:1, kvc * 128:(kvc + 1) * 128], mslice,
                                       start=False, stop=True)
                        TEN.matmul(pka[:, :], ncs_kA[0:1, :], mslice, start=False, stop=True)
                        last = TEN.matmul(pkb[:, :], ncs_kB[0:1, :], mslice, start=False, stop=True)
                        tmm = bld.inc(last, "tT")
                        bld.wait(VEC, tmm)
                        if tq == 0:
                            bld.wait(VEC, t_tabk)
                        rsl = r_b[:, tq * 512:(tq + 1) * 512]
                        for kvc in range(KVC):
                            VEC.tensor_tensor(xkvc_sb[:, kvc * T + tq * 512: kvc * T + (tq + 1) * 512],
                                              pkv[:, kvc * 512:(kvc + 1) * 512], rsl, ALU.mult)
                        VEC.tensor_tensor(tmpa[:], pka[:, :], rsl[0:64, :], ALU.mult)
                        VEC.tensor_tensor(tmpa[:], tmpa[:], cck_sb[:, tq * 512:(tq + 1) * 512],
                                          ALU.mult)
                        VEC.tensor_tensor(tmpb[:], pkb[:, :], rsl[0:64, :], ALU.mult)
                        VEC.tensor_tensor(tmpb[:], tmpb[:], ssk_sb[:, tq * 512:(tq + 1) * 512],
                                          ALU.mult)
                        last = VEC.tensor_tensor(krope_sb[:, tq * 512:(tq + 1) * 512],
                                                 tmpa[:], tmpb[:], ALU.add)
                        tv = bld.inc(last, "tV")
                        bld.wait(TEN, tv)
                with ExitStack() as ps_:
                    pss = ps_.enter_context(nc.psum_tensor("pssk", [1, T], F32))
                    frees = [bld.tok("tV"), bld.tok("tV")]
                    for kvc in range(KVC):
                        for th2 in range(2):
                            sl = (2 * kvc + th2) % 2
                            bld.wait(SCA, frees[sl])
                            tc = bld.inc(SCA.activation(
                                sqb[sl][:],
                                xkvc_sb[:, kvc * T + th2 * 1024: kvc * T + (th2 + 1) * 1024],
                                AF.Square), "tC")
                            bld.wait(TEN, tc)
                            last = None
                            for tt in range(2):
                                last = TEN.matmul(
                                    pss[0:1, th2 * 1024 + tt * 512: th2 * 1024 + (tt + 1) * 512],
                                    ones_b[:], sqb[sl][:, tt * 512:(tt + 1) * 512],
                                    start=(kvc == 0), stop=(kvc == KVC - 1))
                            frees[sl] = bld.inc(last, "tT")
                    bld.wait(VEC, frees[0])
                    bld.wait(VEC, frees[1])
                    tv = bld.inc(VEC.tensor_scalar(rowk[0:1, 0:T], pss[0:1, :], 1.0 / KVR,
                                                   None, ALU.mult), "tV")
                    bld.wait(SCA, tv)
                    SCA.activation(rowk[0:1, 0:T], rowk[0:1, 0:T], AF.Ln, bias=eps_rms[:])
                    tc = bld.inc(SCA.activation(rowk[0:1, 0:T], rowk[0:1, 0:T],
                                                AF.Exp, scale=-0.5), "tC")
                    bld.wait(TEN, tc)
                with nc.psum_tensor("psbk", [128, T], F32) as psb:
                    last = None
                    for tt in range(4):
                        last = TEN.matmul(psb[:, tt * 512:(tt + 1) * 512], ones_fr[:],
                                          rowk[0:1, tt * 512:(tt + 1) * 512], start=True, stop=True)
                    tmm = bld.inc(last, "tT")
                    bld.wait(VEC, tmm)
                    VEC.tensor_copy(rkv_b[:], psb[:])
                    last = None
                    for kvc in range(KVC):
                        last = VEC.scalar_tensor_tensor(normkv_sb[:, kvc * T:(kvc + 1) * T],
                                                        xkvc_sb[:, kvc * T:(kvc + 1) * T],
                                                        kvs_sb[:, kvc:kvc + 1], rkv_b[:],
                                                        ALU.mult, ALU.mult)
                    tv = bld.inc(last, "tV")
                    bld.wait(TEN, tv)
            tap("norm_kv", normkv_sb[:], (128, KVC * T))
            tap("krope", krope_sb[0:64, :], (64, T))
            lvW.close()

            dma_fence()
            lvAA = ExitStack()
            aall_sb = lvAA.enter_context(
                nc.sbuf_tensor("aall", [128, NH * 1024], BF16, side="right"))
            with ExitStack() as st_:
                S_ = st_.enter_context
                K2 = [S_(nc.sbuf_tensor(f"K{i}", [128, T], BF16)) for i in range(2)]
                V2 = [S_(nc.sbuf_tensor(f"V{i}", [128, T], BF16)) for i in range(2)]
                Vt_sb = S_(nc.sbuf_tensor("Vt", [128, T], BF16))
                Qn2 = [S_(nc.sbuf_tensor(f"Qn{i}", [128, NOWN], BF16)) for i in range(2)]
                Qr2 = [S_(nc.sbuf_tensor(f"Qr{i}", [128, NOWN], BF16)) for i in range(2)]
                Qro2 = [S_(nc.sbuf_tensor(f"Qro{i}", [64, NOWN], BF16)) for i in range(2)]
                P2 = [S_(nc.sbuf_tensor("P0", [128, T - 256], BF16)),
                      S_(nc.sbuf_tensor("P1", [128, T], BF16))]
                PT_sb = S_(nc.sbuf_tensor("PT", [128, 16 * 512], BF16))
                negm_sb = S_(nc.sbuf_tensor("negm", [128, NSLOT], F32))
                mtmp_sb = S_(nc.sbuf_tensor("mtmp", [128, 4], F32))
                l_sb = S_(nc.sbuf_tensor("l", [128, NSLOT], F32))
                l2_sb = S_(nc.sbuf_tensor("l2", [128, NSLOT], F32))
                linv_sb = S_(nc.sbuf_tensor("linv", [128, NSLOT], F32))
                tmpA = S_(nc.sbuf_tensor("qtmpA", [128, NOWN], BF16))
                wqn_b = [S_(nc.sbuf_tensor(f"wqn{i}", [128, QR], BF16)) for i in range(2)]
                wqra_b = S_(nc.sbuf_tensor("wqra", [128, QR], BF16))
                wqrb_b = S_(nc.sbuf_tensor("wqrb", [128, QR], BF16))
                wdk_b = [S_(nc.sbuf_tensor(f"wdk{i}", [128, KVR], BF16)) for i in range(2)]
                wdv_b = [S_(nc.sbuf_tensor(f"wdv{i}", [128, KVR], BF16)) for i in range(2)]

                S_ps = S_(nc.psum_tensor("S_ps", [128, T], F32))
                A_ps = S_(nc.psum_tensor("A_ps", [128, 512], F32))
                PT_ps = S_(nc.psum_tensor("PT_ps", [128, 1024], BF16))
                pprod = S_(nc.psum_tensor("pprod", [128, 1024], F32))

                bld.wait(TEN, t_const)
                bld.wait(VEC, t_ccq)
                bld.wait(VEC, t_smask)

                wq_free = [None, None]
                wqa_free = None
                wqb_free = None
                wk_free = [None, None]
                wv_free = [None, None]
                t_wqn = [None, None]
                t_wqa = [None]
                t_wqb = [None]
                t_wdk = [None, None]
                t_wdv = [None, None]

                def issue_head_loads(h):
                    sl = h % 2
                    t_wqn[sl] = dma_g(f"wqn{sl}", wqn_b[sl][:], wqnD[h], free_tok=wq_free[sl])
                    t_wdk[sl] = dma_g(f"wdk{sl}", wdk_b[sl][:], wdkD[h], free_tok=wk_free[sl])
                    t_wdv[sl] = dma_g(f"wdv{sl}", wdv_b[sl][:], wdvD[h], free_tok=wv_free[sl])
                    if h % 2 == 0:
                        t_wqa[0] = dma_g("wqra", wqra_b[:], wqrAD[h // 2], free_tok=wqa_free)
                        t_wqb[0] = dma_g("wqrb", wqrb_b[:], wqrBD[h // 2], free_tok=wqb_free)

                issue_head_loads(0)
                issue_head_loads(1)

                kv_half_tok = [None, None]
                tok_qn = [None, None]
                tok_qr = [None, None]
                tok_qro = [None, None]
                tok_k = [None, None]
                tok_v = [None, None]
                t_ptev = None
                prod_pend = []

                def prod_sub(h, c, part):
                    nonlocal t_ptev, wqa_free, wqb_free
                    p = h % 2
                    if c == 0:
                        tt = part
                        bld.wait(TEN, t_wqn[p])
                        if part == 0:
                            bld.wait(TEN, tok_qn[p])
                            bld.wait(TEN, kv_half_tok[0])
                            bld.wait(TEN, kv_half_tok[1])
                        last = None
                        for qrc in range(QRC):
                            last = TEN.matmul(
                                pprod[:, tt * 512:(tt + 1) * 512],
                                wqn_b[p][:, qrc * 128:(qrc + 1) * 128],
                                normq_sb[:, qrc * 1024 + tt * 512: qrc * 1024 + (tt + 1) * 512],
                                start=(qrc == 0), stop=(qrc == QRC - 1))
                        if part == 1:
                            wq_free[p] = bld.inc(last, "tT")
                            bld.wait(SCA, wq_free[p])
                            tok_qn[p] = bld.inc(SCA.activation(Qn2[p][:], pprod[:, 0:1024],
                                                               AF.Copy, scale=SCALE), "tC")
                    elif c == 1:
                        pp2 = (h // 2) % 2
                        tt = part
                        bld.wait(TEN, t_wqa[0])
                        if part == 0:
                            bld.wait(TEN, tok_qn[p])
                        last = None
                        for qrc in range(QRC):
                            last = TEN.matmul(
                                pprod[:, tt * 512:(tt + 1) * 512],
                                wqra_b[:, qrc * 128:(qrc + 1) * 128],
                                normq_sb[:, qrc * 1024 + tt * 512: qrc * 1024 + (tt + 1) * 512],
                                start=(qrc == 0), stop=(qrc == QRC - 1))
                        if part == 1:
                            wqa_free = bld.inc(last, "tT")
                            bld.wait(VEC, wqa_free)
                            tok_qr[pp2] = bld.inc(VEC.tensor_tensor(tmpA[:], pprod[:, 0:1024],
                                                                    ccq_sb[:], ALU.mult), "tV")
                            bld.wait(TEN, tok_qr[pp2])
                    elif c == 2:
                        pp2 = (h // 2) % 2
                        tt = part
                        bld.wait(TEN, t_wqb[0])
                        last = None
                        for qrc in range(QRC):
                            last = TEN.matmul(
                                pprod[:, tt * 512:(tt + 1) * 512],
                                wqrb_b[:, qrc * 128:(qrc + 1) * 128],
                                normq_sb[:, qrc * 1024 + tt * 512: qrc * 1024 + (tt + 1) * 512],
                                start=(qrc == 0), stop=(qrc == QRC - 1))
                        if part == 1:
                            wqb_free = bld.inc(last, "tT")
                            bld.wait(VEC, wqb_free)
                            bld.wait(VEC, tok_qro[pp2])
                            VEC.tensor_tensor(Qr2[pp2][:], pprod[:, 0:1024], ssq_sb[:], ALU.mult)
                            tok_qr[pp2] = bld.inc(VEC.tensor_tensor(Qr2[pp2][:], tmpA[:],
                                                                    Qr2[pp2][:], ALU.add), "tV")
                            bld.wait(TEN, tok_qr[pp2])
                            tok_qro[pp2] = dma_g(f"qro{pp2}", Qro2[pp2][:], Qr2[pp2][64:128, :],
                                                 free_tok=[tok_qr[pp2], bld.tok("tT")])
                    elif c in (3, 4):
                        tt = 2 * (c - 3) + part
                        bld.wait(TEN, t_wdk[p])
                        if tt == 0:
                            bld.wait(TEN, tok_qn[p])
                        pp = pprod[:, (tt % 2) * 512:((tt % 2) + 1) * 512]
                        bld.wait(TEN, kv_half_tok[tt % 2])
                        last = None
                        for kvc in range(KVC):
                            last = TEN.matmul(pp, wdk_b[p][:, kvc * 128:(kvc + 1) * 128],
                                              normkv_sb[:, kvc * T + tt * 512: kvc * T + (tt + 1) * 512],
                                              start=(kvc == 0), stop=(kvc == KVC - 1))
                        tmm = bld.inc(last, "tT")
                        bld.wait(VEC, tmm)
                        kv_half_tok[tt % 2] = bld.inc(
                            VEC.tensor_copy(K2[p][:, tt * 512:(tt + 1) * 512], pp), "tV")
                        if tt == 3:
                            tok_k[p] = kv_half_tok[1]
                            wk_free[p] = kv_half_tok[1]
                    elif c in (5, 6):
                        tt = 2 * (c - 5) + part
                        bld.wait(TEN, t_wdv[p])
                        pp = pprod[:, (tt % 2) * 512:((tt % 2) + 1) * 512]
                        bld.wait(TEN, kv_half_tok[tt % 2])
                        last = None
                        for kvc in range(KVC):
                            last = TEN.matmul(pp, wdv_b[p][:, kvc * 128:(kvc + 1) * 128],
                                              normkv_sb[:, kvc * T + tt * 512: kvc * T + (tt + 1) * 512],
                                              start=(kvc == 0), stop=(kvc == KVC - 1))
                        tmm = bld.inc(last, "tT")
                        bld.wait(VEC, tmm)
                        kv_half_tok[tt % 2] = bld.inc(
                            VEC.tensor_copy(Vt_sb[:, tt * 512:(tt + 1) * 512], pp), "tV")
                        if tt == 3:
                            wv_free[p] = kv_half_tok[1]
                    elif c == 7:
                        g8 = part
                        bld.wait(TEN, t_ptev)
                        if g8 == 0:
                            bld.wait(TEN, kv_half_tok[0])
                            bld.wait(TEN, kv_half_tok[1])
                        last = None
                        for j8 in range(8):
                            kt = 8 * g8 + j8
                            last = TEN.transpose(PT_ps[:, j8 * 128:(j8 + 1) * 128],
                                                 Vt_sb[:, kt * 128:(kt + 1) * 128], ident_sb[:])
                        tmm = bld.inc(last, "tT")
                        if g8 % 2 == 0:
                            bld.wait(VEC, tmm)
                            t_ptev = bld.inc(
                                VEC.tensor_copy(V2[p][:, g8 * 1024:(g8 + 1) * 1024],
                                                PT_ps[:, :]), "tV")
                        else:
                            bld.wait(SCA, tmm)
                            t_ptev = bld.inc(
                                SCA.activation(V2[p][:, g8 * 1024:(g8 + 1) * 1024],
                                               PT_ps[:, :], AF.Copy), "tC")
                            tok_v[p] = t_ptev

                prod_q = []
                prod_tgt = [0]

                def fill_queue(h):
                    for c in range(8):
                        if c in (1, 2) and h % 2 != 0:
                            continue
                        for part in range(2):
                            prod_q.append((h, c, part))

                def set_slot_target(s):
                    n0 = 16
                    prod_tgt[0] = max(0, n0 * (7 - s) // 8)

                def pop_prod():
                    if prod_q and len(prod_q) > prod_tgt[0]:
                        prod_sub(*prod_q.pop(0))

                def drain_prod():
                    while prod_q:
                        prod_sub(*prod_q.pop(0))

                fill_queue(0)
                drain_prod()

                t_exp_prev = None
                t_exp_c0 = None
                pt_evac_tok = [[] for _ in range(NSLOT)]
                t_psc = [None] * NSLOT

                def s_mms(h, s):
                    nonlocal t_exp_prev, t_exp_c0
                    p = h % 2
                    kext = 256 * (s + 1)
                    bld.wait(TEN, t_exp_c0)
                    if s == 0:
                        bld.wait(TEN, tok_qn[p])
                        bld.wait(TEN, tok_k[p])
                        pp2 = (h // 2) % 2
                        if h % 2 == 0:
                            bld.wait(TEN, tok_qr[pp2])
                        else:
                            bld.wait(TEN, tok_qro[pp2])
                    pp2 = (h // 2) % 2
                    qr_l = Qr2[pp2] if h % 2 == 0 else Qro2[pp2]
                    nkc = (kext + 511) // 512
                    for kc in range(nkc):
                        if kc == 1:
                            bld.wait(TEN, t_exp_prev)
                        n = min(512, kext - kc * 512)
                        TEN.matmul(S_ps[:, kc * 512: kc * 512 + n],
                                   Qn2[p][:, s * 128:(s + 1) * 128],
                                   K2[p][:, kc * 512: kc * 512 + n],
                                   start=True, stop=False)
                        t_kc = bld.inc(TEN.matmul(S_ps[:, kc * 512: kc * 512 + n],
                                                  qr_l[0:64, s * 128:(s + 1) * 128],
                                                  krope_sb[:, kc * 512: kc * 512 + n],
                                                  start=False, stop=True), "tT")
                        bld.wait(VEC, t_kc)
                        VEC.reduce_max(mtmp_sb[:, kc:kc + 1],
                                       S_ps[:, kc * 512: kc * 512 + n], axis=AX.X)
                    t_max = bld.inc(VEC.reduce_max(negm_sb[:, s:s + 1], mtmp_sb[:, 0:nkc],
                                                   axis=AX.X, negate=True), "tV")
                    t_msk = bld.inc(VEC.tensor_tensor(S_ps[:, kext - 256:kext],
                                                      S_ps[:, kext - 256:kext],
                                                      smask_sb[:], ALU.add), "tV")
                    if kext <= 512:
                        bld.wait(SCA, t_msk)
                        t_exp_prev = bld.inc(
                            SCA.activation(P2[s % 2][:, 0:kext], S_ps[:, 0:kext], AF.Exp,
                                           bias=negm_sb[:, s:s + 1],
                                           accum_out=l_sb[:, s:s + 1]), "tC")
                        t_exp_c0 = t_exp_prev
                    else:
                        bld.wait(SCA, t_max)
                        t_exp_c0 = bld.inc(
                            SCA.activation(P2[s % 2][:, 0:512], S_ps[:, 0:512], AF.Exp,
                                           bias=negm_sb[:, s:s + 1],
                                           accum_out=l_sb[:, s:s + 1]), "tC")
                        bld.wait(SCA, t_msk)
                        t_exp_prev = bld.inc(
                            SCA.activation(P2[s % 2][:, 512:kext], S_ps[:, 512:kext], AF.Exp,
                                           bias=negm_sb[:, s:s + 1],
                                           accum_out=l2_sb[:, s:s + 1]), "tC")

                def s_finish(h, s):
                    kext = 256 * (s + 1)
                    tl = t_exp_prev
                    if kext > 512:
                        tl = bld.inc(SCA.activation(l_sb[:, s:s + 1], l_sb[:, s:s + 1],
                                                    AF.Identity,
                                                    bias=l2_sb[:, s:s + 1]), "tC")
                    bld.wait(VEC, tl)
                    VEC.reciprocal(linv_sb[:, s:s + 1], l_sb[:, s:s + 1])
                    VEC.engine_nop()
                    tv = bld.inc(VEC.engine_nop(), "tV")
                    bld.wait(SCA, tv)
                    t_psc[s] = bld.inc(
                        SCA.activation(P2[s % 2][:, 0:kext], P2[s % 2][:, 0:kext], AF.Copy,
                                       scale=linv_sb[:, s:s + 1]), "tC")

                def p_transposes(h, s, mid=None):
                    nonlocal t_ptev
                    kext = 256 * (s + 1)
                    nkt = kext // 128
                    si = s % 4
                    pt_evac_tok[s] = []
                    bld.wait(TEN, t_psc[s])
                    for a in range((nkt + 7) // 8):
                        if a >= 1 and mid is not None:
                            mid()
                        bld.wait(TEN, t_ptev)
                        nb = min(8, nkt - 8 * a)
                        last = None
                        for bji in range(nb):
                            kt = 8 * a + bji
                            last = TEN.transpose(PT_ps[:, bji * 128:(bji + 1) * 128],
                                                 P2[s % 2][:, kt * 128:(kt + 1) * 128],
                                                 ident_sb[:])
                        tmm = bld.inc(last, "tT")
                        dst = PT_sb[:].rearrange("p (k q) -> p k q", q=512)[
                            :, 8 * a:8 * a + nb, si * 128:(si + 1) * 128]
                        srcp = PT_ps[:].rearrange("p (k q) -> p k q", q=128)[:, 0:nb, :]
                        if a % 2 == 0:
                            bld.wait(VEC, tmm)
                            t_ptev = bld.inc(VEC.tensor_copy(dst, srcp), "tV")
                        else:
                            bld.wait(SCA, tmm)
                            t_ptev = bld.inc(SCA.activation(dst, srcp, AF.Copy), "tC")
                        pt_evac_tok[s].append(t_ptev)

                def av_group(h, grp):
                    p = h % 2
                    nkt_g = 8 if grp == 0 else 16
                    bld.wait(TEN, tok_v[p])
                    for s in range(grp * 4, grp * 4 + 4):
                        bld.wait(TEN, pt_evac_tok[s])
                    last = None
                    for kt in range(nkt_g):
                        st0 = (kt // 2) * 128 if grp == 0 else max(0, kt // 2 - 4) * 128
                        last = TEN.matmul(A_ps[:, st0:512],
                                          V2[p][:, kt * 128:(kt + 1) * 128],
                                          PT_sb[:, kt * 512 + st0: kt * 512 + 512],
                                          start=(kt == 0), stop=(kt == nkt_g - 1))
                    t_av = bld.inc(last, "tT")
                    bld.wait(VEC, t_av)
                    t_aev = bld.inc(VEC.tensor_copy(
                        aall_sb[:, h * 1024 + grp * 512: h * 1024 + (grp + 1) * 512],
                        A_ps[:, :]), "tV")
                    bld.wait(TEN, t_aev)

                for h in range(NH):
                    if h + 2 < NH:
                        issue_head_loads(h + 2)
                    if h + 1 < NH:
                        fill_queue(h + 1)
                    for s in range(NSLOT):
                        set_slot_target(s)
                        s_mms(h, s)
                        pop_prod()
                        if s >= 1:
                            p_transposes(h, s - 1, pop_prod)
                        elif h >= 1:
                            p_transposes(h - 1, 7, pop_prod)
                        pop_prod()
                        if s == 0 and h >= 1:
                            av_group(h - 1, 1)
                        if s == 4:
                            av_group(h, 0)
                        s_finish(h, s)
                    drain_prod()
                p_transposes(NH - 1, 7)
                av_group(NH - 1, 1)
                tap("A_all", aall_sb[:], (128, NH * 1024))
                tap("K_last", K2[1][:], (128, T))
                tap("V_last", V2[1][:], (128, T))
                tap("Qn_last", Qn2[1][:], (128, NOWN))
                tap("P_last", P2[1][:], (128, T))
                tap("negm", negm_sb[:], (128, NSLOT))
                tap("l", l_sb[:], (128, NSLOT))
            cbar()
            dma_fence()
            lvP2.close()
            xf_sb = A_(nc.sbuf_tensor("xf", [128, HC * 1024], BF16))
            xf_tok = []
            pre_norm_fence = [None, None, None]

            dma_fence()
            with ExitStack() as st_:
                S_ = st_.enter_context
                wo_b = [S_(nc.sbuf_tensor(f"wo{i}", [128, NH * 128], BF16)) for i in range(2)]
                xo_b = [S_(nc.sbuf_tensor(f"xo{i}", [128, 1024], F32)) for i in range(2)]
                sqb = [S_(nc.sbuf_tensor(f"fsq{i}", [128, 1024], BF16)) for i in range(2)]
                xbb = [S_(nc.sbuf_tensor(f"fxb{i}", [128, 1024], BF16)) for i in range(2)]
                p45 = st_.enter_context(ExitStack())
                py = p45.enter_context(nc.psum_tensor("py", [128, 2 * 1024], F32))
                psx = p45.enter_context(nc.psum_tensor("fpsx", [1, 1024], F32))
                psq = p45.enter_context(nc.psum_tensor("fpsq", [1, 1024], F32))
                wfree = [None, None]
                xfree = [None, None]
                evtok = [None, None]
                stfree = [None, None]
                for oc in range(HC):
                    sl = oc % 2
                    t_w = dma_g(f"wo{sl}", wo_b[sl][:], woD[oc], free_tok=wfree[sl])
                    t_x = dma_s(f"xo{sl}", xo_b[sl][:], xownT[oc * 128:(oc + 1) * 128, :],
                                free_tok=xfree[sl])
                    pp = py[:, sl * 1024:(sl + 1) * 1024]
                    bld.wait(TEN, t_w)
                    bld.wait(TEN, evtok[sl])
                    last = None
                    for vdc in range(NH):
                        for tt in range(2):
                            last = TEN.matmul(
                                pp[:, tt * 512:(tt + 1) * 512],
                                wo_b[sl][:, vdc * 128:(vdc + 1) * 128],
                                aall_sb[:, vdc * 1024 + tt * 512: vdc * 1024 + (tt + 1) * 512],
                                start=(vdc == 0), stop=(vdc == NH - 1))
                    tmm = bld.inc(last, "tT")
                    wfree[sl] = tmm
                    bld.wait(VEC, tmm)
                    bld.wait(VEC, t_x)
                    bld.wait(VEC, stfree[sl])
                    tv = bld.inc(VEC.tensor_tensor(x1_sb[:, oc * 1024:(oc + 1) * 1024],
                                                   pp, xo_b[sl][:], ALU.add), "tV")
                    xfree[sl] = tv
                    evtok[sl] = tv
                    VEC.tensor_copy(xbb[sl][:], x1_sb[:, oc * 1024:(oc + 1) * 1024])
                    tvb = bld.inc(VEC.engine_nop(), "tV")
                    bld.wait(SCA, tv)
                    bld.wait(SCA, stfree[sl])
                    tc = bld.inc(SCA.activation(sqb[sl][:],
                                                x1_sb[:, oc * 1024:(oc + 1) * 1024],
                                                AF.Square), "tC")
                    bld.wait(TEN, tvb)
                    bld.wait(TEN, tc)
                    last = None
                    for tt in range(2):
                        TEN.matmul(psx[0:1, tt * 512:(tt + 1) * 512], ones_b[:],
                                   xbb[sl][:, tt * 512:(tt + 1) * 512],
                                   start=(oc == 0), stop=(oc == HC - 1))
                        last = TEN.matmul(psq[0:1, tt * 512:(tt + 1) * 512], ones_b[:],
                                          sqb[sl][:, tt * 512:(tt + 1) * 512],
                                          start=(oc == 0), stop=(oc == HC - 1))
                    stfree[sl] = bld.inc(last, "tT")
                bld.wait(TEN, evtok[0])
                bld.wait(TEN, evtok[1])
                with ExitStack() as st5x:
                    rowf = st5x.enter_context(nc.sbuf_tensor("rowf", [1, 2 * 1024], F32))
                    bld.wait(VEC, stfree[0])
                    bld.wait(VEC, stfree[1])
                    W = 1024
                    VEC.tensor_scalar(rowf[0:1, 0:W], psx[0:1, :], 1.0 / H, None, ALU.mult)
                    VEC.tensor_scalar(rowf[0:1, W:2 * W], psq[0:1, :], 1.0 / H, None, ALU.mult)
                    VEC.scalar_tensor_tensor(psx[0:1, :], rowf[0:1, 0:W], -1.0,
                                             rowf[0:1, 0:W], ALU.mult, ALU.mult)
                    tv = bld.inc(VEC.tensor_tensor(rowf[0:1, W:2 * W], rowf[0:1, W:2 * W],
                                                   psx[0:1, :], ALU.add), "tV")
                    bld.wait(SCA, tv)
                    SCA.activation(rowf[0:1, W:2 * W], rowf[0:1, W:2 * W], AF.Ln,
                                   bias=eps_ln[:])
                    tc = bld.inc(SCA.activation(rowf[0:1, W:2 * W], rowf[0:1, W:2 * W],
                                                AF.Exp, scale=-0.5), "tC")
                    bld.wait(VEC, tc)
                    VEC.tensor_tensor(psx[0:1, :], rowf[0:1, 0:W], rowf[0:1, W:2 * W],
                                      ALU.mult)
                    tv = bld.inc(VEC.tensor_copy(rowf[0:1, 0:W], psx[0:1, :]), "tV")
                    bld.wait(TEN, tv)
                    p45.close()
                    with ExitStack() as st5y:
                        psb = st5y.enter_context(nc.psum_tensor("fpsb", [128, 2 * 1024], F32))
                        rf_b = st5y.enter_context(nc.sbuf_tensor("rf_b", [128, 1024], F32))
                        mrf_b = st5y.enter_context(nc.sbuf_tensor("mrf_b", [128, 1024], F32))
                        ftmp = st5y.enter_context(nc.sbuf_tensor("ftmp", [128, 1024], F32))
                        W = 1024
                        last = None
                        for tt in range(2):
                            TEN.matmul(psb[:, tt * 512:(tt + 1) * 512], ones_fr[:],
                                       rowf[0:1, W + tt * 512: W + (tt + 1) * 512],
                                       start=True, stop=True)
                            last = TEN.matmul(psb[:, W + tt * 512: W + (tt + 1) * 512],
                                              ones_fr[:], rowf[0:1, tt * 512:(tt + 1) * 512],
                                              start=True, stop=True)
                        tmm = bld.inc(last, "tT")
                        bld.wait(VEC, tmm)
                        VEC.tensor_copy(rf_b[:], psb[:, 0:W])
                        VEC.tensor_copy(mrf_b[:], psb[:, W:2 * W])
                        pre_norm_fence[:] = [bld.tok("tV"), bld.tok("tC"), bld.tok("tT")]
                        xf_tok.clear()
                        for oc in range(HC):
                            VEC.tensor_tensor(ftmp[:], x1_sb[:, oc * 1024:(oc + 1) * 1024],
                                              rf_b[:], ALU.mult)
                            xf_tok.append(bld.inc(
                                VEC.tensor_tensor(xf_sb[:, oc * 1024:(oc + 1) * 1024],
                                                  ftmp[:], mrf_b[:], ALU.subtract), "tV"))
            cbar()
            lvAA.close()
            tap("x1", x1_sb[:], (128, HC * 1024))
            tap("xf", xf_sb[:], (128, HC * 1024))

            bld.wait(GP, pre_norm_fence[0])
            bld.wait(GP, pre_norm_fence[1])
            bld.wait(GP, pre_norm_fence[2])
            bld.wait(SY, pre_norm_fence[0])
            bld.wait(SY, pre_norm_fence[1])
            bld.wait(SY, pre_norm_fence[2])
            bld.wait(SCA, t_const)
            with ExitStack() as f6:
                S6 = f6.enter_context
                h_sb = S6(nc.sbuf_tensor("hh", [128, 32 * 1024], BF16))
                gt_b = [S6(nc.sbuf_tensor(f"gt{i}", [128, HC * 128], BF16)) for i in range(2)]
                ut_b = [S6(nc.sbuf_tensor(f"ut{i}", [128, HC * 128], BF16)) for i in range(2)]
                dt_b = [S6(nc.sbuf_tensor(f"dt{i}", [128, 32 * 128], BF16)) for i in range(2)]
                sg_b = [S6(nc.sbuf_tensor(f"sg{i}", [128, 1024], BF16)) for i in range(2)]
                oc_b = [S6(nc.sbuf_tensor(f"oc{i}", [128, 1024], F32)) for i in range(2)]
                pg = [S6(nc.psum_tensor(f"pg{i}", [128, 1024], F32)) for i in range(2)]
                pu = [S6(nc.psum_tensor(f"pu{i}", [128, 1024], F32)) for i in range(2)]
                gfree = [None, None]
                ufree = [None, None]
                dfree = [None, None]
                ocfree = [None, None]
                hread = None
                for half in range(2):
                    t_hmul = [None, None]
                    t_silu = [None, None]
                    for ib in range(32):
                        ibg = half * 32 + ib
                        sl = ibg % 2
                        r3 = ibg % 2
                        t_g = dma_g(f"gt{r3}", gt_b[r3][:], gwD[ibg], free_tok=gfree[r3])
                        t_u = dma_g(f"ut{r3}", ut_b[r3][:], uwD[ibg], free_tok=ufree[r3])
                        bld.wait(SCA, t_g)
                        SCA.activation(gt_b[r3][:], gt_b[r3][:], AF.Identity, bias=n128[:])
                        bld.wait(SCA, t_u)
                        t_sh = bld.inc(SCA.activation(ut_b[r3][:], ut_b[r3][:], AF.Identity,
                                                      bias=n128[:]), "tC")
                        bld.wait(TEN, t_sh)
                        bld.wait(TEN, t_hmul[sl])
                        last = None
                        for tt in range(2):
                            for hcc in range(HC):
                                if ibg == 0 and xf_tok:
                                    bld.wait(TEN, xf_tok[hcc])
                                last = TEN.matmul(
                                    pg[sl][:, tt * 512:(tt + 1) * 512],
                                    gt_b[r3][:, hcc * 128:(hcc + 1) * 128],
                                    xf_sb[:, hcc * 1024 + tt * 512: hcc * 1024 + (tt + 1) * 512],
                                    start=(hcc == 0), stop=(hcc == HC - 1))
                        t_gmm = bld.inc(last, "tT")
                        gfree[r3] = t_gmm
                        last = None
                        for tt in range(2):
                            for hcc in range(HC):
                                last = TEN.matmul(
                                    pu[sl][:, tt * 512:(tt + 1) * 512],
                                    ut_b[r3][:, hcc * 128:(hcc + 1) * 128],
                                    xf_sb[:, hcc * 1024 + tt * 512: hcc * 1024 + (tt + 1) * 512],
                                    start=(hcc == 0), stop=(hcc == HC - 1))
                        t_umm = bld.inc(last, "tT")
                        ufree[r3] = t_umm
                        bld.wait(SCA, t_gmm)
                        bld.wait(SCA, t_hmul[sl])
                        t_silu[sl] = bld.inc(SCA.activation(sg_b[sl][:], pg[sl][:], AF.Silu,
                                                            scale=gs_sb[:, ibg:ibg + 1]), "tC")
                        bld.wait(VEC, t_silu[sl])
                        bld.wait(VEC, t_umm)
                        bld.wait(VEC, hread)
                        t_hmul[sl] = bld.inc(
                            VEC.scalar_tensor_tensor(h_sb[:, ib * 1024:(ib + 1) * 1024],
                                                     pu[sl][:], us_sb[:, ibg:ibg + 1], sg_b[sl][:],
                                                     ALU.mult, ALU.mult), "tV")
                    t_hall = bld.tok("tV")
                    dev = [None, None]
                    for oc in range(HC):
                        sl = oc % 2
                        t_d = dma_g(f"dt{sl}", dt_b[sl][:],
                                    dwD[oc, :, half * 32:(half + 1) * 32, :], free_tok=dfree[sl])
                        bld.wait(SCA, t_d)
                        t_sh = bld.inc(SCA.activation(dt_b[sl][:], dt_b[sl][:], AF.Identity,
                                                      bias=n128[:]), "tC")
                        pp = pg[sl]
                        bld.wait(TEN, t_sh)
                        bld.wait(TEN, t_hall)
                        bld.wait(TEN, t_hmul[sl])
                        bld.wait(TEN, dev[sl])
                        last = None
                        for tt in range(2):
                            for ib in range(32):
                                last = TEN.matmul(
                                    pp[:, tt * 512:(tt + 1) * 512],
                                    dt_b[sl][:, ib * 128:(ib + 1) * 128],
                                    h_sb[:, ib * 1024 + tt * 512: ib * 1024 + (tt + 1) * 512],
                                    start=(ib == 0), stop=(ib == 31))
                        tmm = bld.inc(last, "tT")
                        dfree[sl] = tmm
                        bld.wait(VEC, tmm)
                        if half == 0:
                            tv = bld.inc(VEC.scalar_tensor_tensor(
                                x1_sb[:, oc * 1024:(oc + 1) * 1024], pp[:],
                                ds_sb[:, oc:oc + 1], x1_sb[:, oc * 1024:(oc + 1) * 1024],
                                ALU.mult, ALU.add), "tV")
                            dev[sl] = tv
                        else:
                            bld.wait(VEC, ocfree[sl])
                            tv = bld.inc(VEC.scalar_tensor_tensor(
                                oc_b[sl][:], pp[:], ds_sb[:, oc:oc + 1],
                                x1_sb[:, oc * 1024:(oc + 1) * 1024],
                                ALU.mult, ALU.add), "tV")
                            dev[sl] = tv
                            ocfree[sl] = dma_s(f"out{sl}", outT[oc * 128:(oc + 1) * 128, :],
                                               oc_b[sl][:], free_tok=tv)
                    hread = bld.tok("tT")
        SY.wait_ge(bld.sem("s_out0"), bld.cnt["s_out0"])
        SY.wait_ge(bld.sem("s_out1"), bld.cnt["s_out1"])
        nc.all_engine_barrier()
    return nc



_PERM_A = np.concatenate([np.arange(0, 64, 2), np.arange(1, 64, 2)])
_PERM_B = np.concatenate([np.arange(1, 64, 2), np.arange(0, 64, 2)])


def _rope_tables():
    j = np.arange(32, dtype=np.float64)
    inv = 10000.0 ** (-(j * 2.0 / 64.0))
    pos = np.arange(T, dtype=np.float64)
    ang = pos[:, None] * inv[None, :]
    return np.cos(ang), np.sin(ang)


def _stage_shared(inp):
    f32 = np.float32
    s = {}
    s["w_cq"] = np.ascontiguousarray(np.asarray(inp["L0_attn_W_cq"], f32))
    s["w_ckv"] = np.ascontiguousarray(np.asarray(inp["L0_attn_W_ckv"], f32))
    s["qsT"] = np.ascontiguousarray(np.asarray(inp["L0_attn_q_scale"], f32).reshape(QRC, 128).T)
    s["kvsT"] = np.ascontiguousarray(np.asarray(inp["L0_attn_kv_scale"], f32).reshape(KVC, 128).T)
    wqn = np.asarray(inp["L0_attn_W_dq_nope"], f32)
    s["wqn"] = np.ascontiguousarray(wqn.reshape(QRC, 128, NH, 128).transpose(2, 1, 0, 3))
    wr = np.asarray(inp["L0_attn_W_dq_rope"], f32).reshape(QR, NH, 64)
    wra = wr[:, :, _PERM_A].reshape(QR, NH // 2, 128)
    wrb = wr[:, :, _PERM_B].reshape(QR, NH // 2, 128)
    s["wqrA"] = np.ascontiguousarray(wra.reshape(QRC, 128, NH // 2, 128).transpose(2, 1, 0, 3))
    s["wqrB"] = np.ascontiguousarray(wrb.reshape(QRC, 128, NH // 2, 128).transpose(2, 1, 0, 3))
    wdk = np.asarray(inp["L0_attn_W_dk_nope"], f32)
    s["wdk"] = np.ascontiguousarray(wdk.reshape(KVC, 128, NH, 128).transpose(2, 1, 0, 3))
    wdv = np.asarray(inp["L0_attn_W_dv"], f32)
    s["wdv"] = np.ascontiguousarray(wdv.reshape(KVC, 128, NH, 128).transpose(2, 1, 0, 3))
    wkr = np.asarray(inp["L0_attn_W_krope"], f32)
    s["wkrA"] = np.ascontiguousarray(wkr[:, _PERM_A].reshape(HC, 128, 64).transpose(1, 0, 2))
    s["wkrB"] = np.ascontiguousarray(wkr[:, _PERM_B].reshape(HC, 128, 64).transpose(1, 0, 2))
    wo = np.asarray(inp["L0_attn_W_o"], f32)
    s["wo"] = np.ascontiguousarray(wo.reshape(NH, 128, HC, 128).transpose(2, 1, 0, 3))
    def _u8(a):
        a = np.ascontiguousarray(np.asarray(a, np.int32))
        return a.view(np.uint8).reshape(a.shape + (4,))[..., 0]
    s["gw"] = np.ascontiguousarray(
        _u8(inp["L0_ffn_gate_w"]).reshape(NIB, 128, HC, 128).transpose(0, 3, 2, 1))
    s["uw"] = np.ascontiguousarray(
        _u8(inp["L0_ffn_up_w"]).reshape(NIB, 128, HC, 128).transpose(0, 3, 2, 1))
    s["dw"] = np.ascontiguousarray(
        _u8(inp["L0_ffn_down_w"]).reshape(HC, 128, NIB, 128).transpose(0, 3, 2, 1))
    s["gsT"] = np.ascontiguousarray(np.asarray(inp["L0_ffn_gate_scale"], f32).reshape(NIB, 128).T)
    s["usT"] = np.ascontiguousarray(np.asarray(inp["L0_ffn_up_scale"], f32).reshape(NIB, 128).T)
    s["dsT"] = np.ascontiguousarray(np.asarray(inp["L0_ffn_down_scale"], f32).reshape(HC, 128).T)
    s["ident"] = np.eye(128, dtype=ml_dtypes.bfloat16)
    cosT, sinT = _rope_tables()
    s["cck"] = np.ascontiguousarray(np.concatenate([cosT.T, cosT.T], 0) / 16.0).astype(f32)
    s["ssk"] = np.ascontiguousarray(np.concatenate([-sinT.T, sinT.T], 0) / 16.0).astype(f32)
    s["_cosT"], s["_sinT"] = cosT, sinT
    return s


def _stage_core(inp, s, core):
    b, side = core // 2, core % 2
    f32 = np.float32
    d = {}
    x = np.asarray(inp["L0_attn_x"], f32)
    xTb = np.ascontiguousarray(x[b].T)
    d["xT"] = xTb
    d["xownT"] = np.ascontiguousarray(xTb.reshape(H, HC, 128)[:, side::2, :].reshape(H, NOWN))
    ownpos = np.arange(NOWN) // 128 * 256 + side * 128 + np.arange(NOWN) % 128
    cq = s["_cosT"].T[:, ownpos]
    sq = s["_sinT"].T[:, ownpos]
    sc = 1.0 / np.sqrt(np.float64(192.0))
    d["ccq"] = np.ascontiguousarray(np.concatenate([cq, cq, cq, cq], 0) * sc).astype(f32)
    d["ssq"] = np.ascontiguousarray(np.concatenate([-sq, sq, -sq, sq], 0) * sc).astype(f32)
    m = np.zeros((128, 256), f32)
    tri = np.where(np.arange(128)[:, None] >= np.arange(128)[None, :], 0.0, NEGBIG).astype(f32)
    if side == 0:
        m[:, 0:128] = tri
        m[:, 128:256] = NEGBIG
    else:
        m[:, 128:256] = tri
    d["smask"] = m
    return d


def kernel(**inp):
    s = _stage_shared(inp)
    nc = build_program()
    shared = {k: v for k, v in s.items() if not k.startswith("_")}
    in_maps = []
    for c in range(8):
        m = dict(shared)
        m.update(_stage_core(inp, s, c))
        in_maps.append(m)
    res = run_bass_kernel_spmd(nc, in_maps, core_ids=list(range(8)))
    out = np.empty((B, T, H), np.float32)
    for c in range(8):
        b, side = c // 2, c % 2
        sh = np.asarray(res.results[c]["outT"])
        v = sh.reshape(H, NSLOT, 128)
        for sl in range(NSLOT):
            t0 = 128 * (2 * sl + side)
            out[b, t0:t0 + 128, :] = v[:, sl, :].T
    return out


# version 17
# speedup vs baseline: 1.0242x; 1.0242x over previous
import math
import numpy as np
import ml_dtypes
import concourse.bass as bass
import concourse.mybir as mybir
from concourse.bass_utils import run_bass_kernel_spmd

F32 = mybir.dt.float32
BF16 = mybir.dt.bfloat16
I32 = mybir.dt.int32
U8 = mybir.dt.uint8
AF = mybir.ActivationFunctionType
ALU = mybir.AluOpType
AX = mybir.AxisListType

B, T, H = 4, 2048, 2048
HC = 16
NOWN = 1024
QR, QRC = 1024, 8
KVR, KVC = 512, 4
NH = 16
NSLOT = 8
NIB = 64
SCALE = float(1.0 / np.sqrt(np.float64(192.0)))
EPS_LN, EPS_RMS = 1e-5, 1e-6
NEGBIG = -1.0e9


class Bld:
    def __init__(self, nc):
        self.nc = nc
        self.sems = {}
        self.cnt = {}

    def sem(self, name):
        if name not in self.sems:
            self.sems[name] = self.nc.alloc_semaphore(name)
            self.cnt[name] = 0
        return self.sems[name]

    def inc(self, instr, name, k=1):
        instr.then_inc(self.sem(name), k)
        self.cnt[name] += k
        return (name, self.cnt[name])

    def tok(self, name):
        return (name, self.cnt.get(name, 0))

    def wait(self, eng, tok):
        if tok is None:
            return
        if isinstance(tok, list):
            for t in tok:
                self.wait(eng, t)
            return
        eng.wait_ge(self.sem(tok[0]), tok[1])


def build_program(taps=(), detect_races=False):
    nc = bass.Bass("TRN2", target_bir_lowering=False,
                   detect_race_conditions=detect_races)
    TEN, VEC, SCA, GP, SY = nc.tensor, nc.vector, nc.scalar, nc.gpsimd, nc.sync
    bld = Bld(nc)
    ET = mybir.EngineType

    def cbar():
        nc.multi_engine_barrier([ET.PE, ET.Activation, ET.DVE])

    def dma_fence():
        for s in ("tV", "tC", "tT"):
            bld.wait(GP, bld.tok(s))
            bld.wait(SY, bld.tok(s))

    def dma_g(slot, dst, src, free_tok=None):
        bld.wait(GP, free_tok)
        return bld.inc(GP.dma_start(dst, src), "g_" + slot, 16)

    def dma_s(slot, dst, src, free_tok=None):
        bld.wait(SY, free_tok)
        return bld.inc(SY.dma_start(dst, src), "s_" + slot, 16)

    def din(name, shape, dtype):
        return nc.dram_tensor(name, shape, dtype, kind="ExternalInput").ap()

    xT = din("xT", [H, T], F32)
    xownT = din("xownT", [H, NOWN], F32)
    ccqD = din("ccq", [128, NOWN], F32)
    ssqD = din("ssq", [128, NOWN], F32)
    cckD = din("cck", [64, T], F32)
    sskD = din("ssk", [64, T], F32)
    smaskD = din("smask", [128, 256], F32)
    identD = din("ident", [128, 128], BF16)
    qsD = din("qsT", [128, QRC], F32)
    kvsD = din("kvsT", [128, KVC], F32)
    gsD = din("gsT", [128, NIB], F32)
    usD = din("usT", [128, NIB], F32)
    dsD = din("dsT", [128, HC], F32)
    w_cq = din("w_cq", [H, QR], F32)
    w_ckv = din("w_ckv", [H, KVR], F32)
    wkrAD = din("wkrA", [128, HC, 64], F32)
    wkrBD = din("wkrB", [128, HC, 64], F32)
    wqnD = din("wqn", [NH, 128, QRC, 128], F32)
    wqrAD = din("wqrA", [NH // 2, 128, QRC, 128], F32)
    wqrBD = din("wqrB", [NH // 2, 128, QRC, 128], F32)
    wdkD = din("wdk", [NH, 128, KVC, 128], F32)
    wdvD = din("wdv", [NH, 128, KVC, 128], F32)
    woD = din("wo", [HC, 128, NH, 128], F32)
    gwD = din("gw", [NIB, 128, HC, 128], U8)
    uwD = din("uw", [NIB, 128, HC, 128], U8)
    dwD = din("dw", [HC, 128, NIB, 128], U8)
    outT = nc.dram_tensor("outT", [H, NOWN], F32, kind="ExternalOutput").ap()

    def tap(name, sb_ap, shape):
        if name not in taps:
            return
        d = nc.dram_tensor("tap_" + name, list(shape), F32, kind="ExternalOutput").ap()
        cbar()
        for s in ("tV", "tC", "tT"):
            bld.wait(GP, bld.tok(s))
        t = bld.inc(GP.dma_start(d, sb_ap), "g_tap", 16)
        bld.wait(GP, t)

    from contextlib import ExitStack

    with ExitStack() as top:
        E = top.enter_context

        ident_sb = E(nc.sbuf_tensor("identS", [128, 128], BF16))
        smask_sb = E(nc.sbuf_tensor("smaskS", [128, 256], BF16))
        qs_sb = E(nc.sbuf_tensor("qs", [128, QRC], F32))
        kvs_sb = E(nc.sbuf_tensor("kvs", [128, KVC], F32))
        gs_sb = E(nc.sbuf_tensor("gs", [128, NIB], F32))
        us_sb = E(nc.sbuf_tensor("us", [128, NIB], F32))
        ds_sb = E(nc.sbuf_tensor("ds", [128, HC], F32))
        ones_b = E(nc.sbuf_tensor("ones_b", [128, 1], BF16))
        nones_b = E(nc.sbuf_tensor("nones_b", [128, 1], BF16))
        ones_fr = E(nc.sbuf_tensor("ones_fr", [1, 128], F32))
        eps_ln = E(nc.sbuf_tensor("eps_ln", [1, 1], F32))
        eps_rms = E(nc.sbuf_tensor("eps_rms", [1, 1], F32))
        n128 = E(nc.sbuf_tensor("n128", [128, 1], F32))

        VEC.memset(n128[:], -128.0)
        VEC.memset(ones_b[:], 1.0)
        VEC.memset(nones_b[:], -1.0)
        VEC.memset(ones_fr[:], 1.0)
        VEC.memset(eps_ln[:], EPS_LN)
        VEC.memset(eps_rms[:], EPS_RMS)
        dma_s("const", ident_sb[:], identD)
        t_smask = dma_g("constg", smask_sb[:], smaskD)
        dma_s("const", qs_sb[:], qsD)
        dma_s("const", kvs_sb[:], kvsD)
        dma_s("const", gs_sb[:], gsD)
        dma_s("const", us_sb[:], usD)
        t_const = dma_s("const", ds_sb[:], dsD)

        x1_sb = E(nc.sbuf_tensor("x1", [128, HC * 1024], F32, side="right"))

        with ExitStack() as lvA:
            A_ = lvA.enter_context
            lvP2 = lvA.enter_context(ExitStack())
            P2_ = lvP2.enter_context
            normq_sb = P2_(nc.sbuf_tensor("normq", [128, QRC * 1024], BF16))
            normkv_sb = P2_(nc.sbuf_tensor("normkv", [128, KVC * T], BF16))
            krope_sb = P2_(nc.sbuf_tensor("krope", [64, T], BF16))
            ccq_sb = P2_(nc.sbuf_tensor("ccqS", [128, NOWN], BF16))
            ssq_sb = P2_(nc.sbuf_tensor("ssqS", [128, NOWN], BF16))
            lvW = lvA.enter_context(ExitStack())
            W_ = lvW.enter_context
            m_bf = W_(nc.sbuf_tensor("m_bf", [1, T], BF16))
            r_b = W_(nc.sbuf_tensor("r_b", [128, T], F32))
            mo_bf = W_(nc.sbuf_tensor("mo_bf", [1, NOWN], BF16))
            ro_b = W_(nc.sbuf_tensor("ro_b", [128, NOWN], F32))
            ncs_q = W_(nc.sbuf_tensor("ncs_q", [1, QR], BF16))
            ncs_kv = W_(nc.sbuf_tensor("ncs_kv", [1, KVR], BF16))
            ncs_kA = W_(nc.sbuf_tensor("ncs_kA", [1, 64], BF16))
            ncs_kB = W_(nc.sbuf_tensor("ncs_kB", [1, 64], BF16))
            wkrA_sb = W_(nc.sbuf_tensor("wkrAS", [128, HC * 64], BF16))
            wkrB_sb = W_(nc.sbuf_tensor("wkrBS", [128, HC * 64], BF16))

            dma_g("tabq", ccq_sb[:], ccqD)
            t_ccq = dma_g("tabq", ssq_sb[:], ssqD)
            t_wkrA = dma_g("wkrA", wkrA_sb[:], wkrAD)
            t_wkrB = dma_g("wkrB", wkrB_sb[:], wkrBD)

            def ln_stats(src_ap, width, m_out, r_out, pref, dual=False):
                NR = 3
                cw = min(width, 1024)
                with ExitStack() as st_:
                    S_ = st_.enter_context
                    xin = [S_(nc.sbuf_tensor(f"{pref}xin{i}", [128, cw], F32)) for i in range(NR)]
                    xbf = [S_(nc.sbuf_tensor(f"{pref}xbf{i}", [128, cw], BF16)) for i in range(NR)]
                    xsq = [S_(nc.sbuf_tensor(f"{pref}xsq{i}", [128, cw], BF16)) for i in range(NR)]
                    rows = S_(nc.sbuf_tensor(f"{pref}rows", [1, 2 * width], F32))
                    steps = [(hc, c0) for hc in range(HC) for c0 in range(0, width, cw)]
                    with ExitStack() as ps_:
                        psx = ps_.enter_context(nc.psum_tensor(f"{pref}psx", [1, width], F32))
                        psq = ps_.enter_context(nc.psum_tensor(f"{pref}psq", [1, width], F32))
                        free_in = [None] * NR
                        free_mm = [None] * NR
                        for i, (hc, c0) in enumerate(steps):
                            sl = i % NR
                            dq = dma_g if (dual and i % 2 == 1) else dma_s
                            t_x = dq(f"{pref}x{sl}", xin[sl][:],
                                     src_ap[hc * 128:(hc + 1) * 128, c0:c0 + cw],
                                     free_tok=free_in[sl])
                            bld.wait(VEC, t_x)
                            bld.wait(VEC, free_mm[sl])
                            tv = bld.inc(VEC.tensor_copy(xbf[sl][:], xin[sl][:]), "tV")
                            bld.wait(SCA, t_x)
                            bld.wait(SCA, free_mm[sl])
                            tc = bld.inc(SCA.activation(xsq[sl][:], xin[sl][:], AF.Square), "tC")
                            free_in[sl] = [tv, tc]
                            bld.wait(TEN, tv)
                            bld.wait(TEN, tc)
                            last = None
                            for tt in range(cw // 512):
                                TEN.matmul(psx[0:1, c0 + tt * 512:c0 + (tt + 1) * 512], ones_b[:],
                                           xbf[sl][:, tt * 512:(tt + 1) * 512],
                                           start=(hc == 0), stop=(hc == HC - 1))
                                last = TEN.matmul(psq[0:1, c0 + tt * 512:c0 + (tt + 1) * 512],
                                                  ones_b[:], xsq[sl][:, tt * 512:(tt + 1) * 512],
                                                  start=(hc == 0), stop=(hc == HC - 1))
                            free_mm[sl] = bld.inc(last, "tT")
                        bld.wait(VEC, free_mm[0])
                        bld.wait(VEC, free_mm[1])
                        W = width
                        VEC.tensor_scalar(rows[0:1, 0:W], psx[0:1, :], 1.0 / H, None, ALU.mult)
                        VEC.tensor_scalar(rows[0:1, W:2 * W], psq[0:1, :], 1.0 / H, None, ALU.mult)
                        VEC.scalar_tensor_tensor(psx[0:1, :], rows[0:1, 0:W], -1.0,
                                                 rows[0:1, 0:W], ALU.mult, ALU.mult)
                        VEC.tensor_tensor(rows[0:1, W:2 * W], rows[0:1, W:2 * W],
                                          psx[0:1, :], ALU.add)
                        tv = bld.inc(VEC.tensor_copy(m_out[0:1, :], rows[0:1, 0:W]), "tV")
                        bld.wait(TEN, tv)
                    W = width
                    bld.wait(SCA, tv)
                    SCA.activation(rows[0:1, W:2 * W], rows[0:1, W:2 * W], AF.Ln, bias=eps_ln[:])
                    tc = bld.inc(SCA.activation(rows[0:1, W:2 * W], rows[0:1, W:2 * W],
                                                AF.Exp, scale=-0.5), "tC")
                    with nc.psum_tensor(f"{pref}psb", [128, width], F32) as psb:
                        bld.wait(TEN, tc)
                        last = None
                        for tt in range(width // 512):
                            last = TEN.matmul(psb[:, tt * 512:(tt + 1) * 512], ones_fr[:],
                                              rows[0:1, W + tt * 512: W + (tt + 1) * 512],
                                              start=True, stop=True)
                        tmm = bld.inc(last, "tT")
                        bld.wait(VEC, tmm)
                        tv = bld.inc(VEC.tensor_copy(r_out[:, :], psb[:, :]), "tV")
                        bld.wait(TEN, tv)

            ln_stats(xT, T, m_bf, r_b, "lnf", dual=True)
            tap("m_bf", m_bf[:], (1, T))
            tap("r_b", r_b[:], (128, T))

            dma_fence()
            with ExitStack() as st_:
                S_ = st_.enter_context
                xin = [S_(nc.sbuf_tensor(f"lnoxin{i}", [128, NOWN], F32)) for i in range(3)]
                xbf = [S_(nc.sbuf_tensor(f"lnoxbf{i}", [128, NOWN], BF16)) for i in range(3)]
                xsq = [S_(nc.sbuf_tensor(f"lnoxsq{i}", [128, NOWN], BF16)) for i in range(3)]
                rows = S_(nc.sbuf_tensor("lnorows", [1, 2 * NOWN], F32))
                cswb = [S_(nc.sbuf_tensor(f"csw{i}", [128, QR], BF16)) for i in range(2)]
                ckwb = [S_(nc.sbuf_tensor(f"ckw{i}", [128, KVR], BF16)) for i in range(2)]
                with ExitStack() as ps_:
                    psx = ps_.enter_context(nc.psum_tensor("lnopsx", [1, NOWN], F32))
                    psq = ps_.enter_context(nc.psum_tensor("lnopsq", [1, NOWN], F32))
                    pq = ps_.enter_context(nc.psum_tensor("pcsq", [1, QR], F32))
                    pkv = ps_.enter_context(nc.psum_tensor("pcskv", [1, KVR], F32))
                    pkab = ps_.enter_context(nc.psum_tensor("pcskab", [1, 64], F32))
                    free_in = [None] * 3
                    free_mm = [None] * 3
                    freew = [None, None]
                    freekw = [None, None]
                    for hc in range(HC):
                        sl = hc % 3
                        wl = hc % 2
                        t_w = dma_g(f"csw{wl}", cswb[wl][:], w_cq[hc * 128:(hc + 1) * 128, :],
                                    free_tok=freew[wl])
                        t_wk = dma_g(f"ckw{wl}", ckwb[wl][:], w_ckv[hc * 128:(hc + 1) * 128, :],
                                     free_tok=freekw[wl])
                        t_x = dma_s(f"lnox{sl}", xin[sl][:],
                                    xownT[hc * 128:(hc + 1) * 128, :], free_tok=free_in[sl])
                        bld.wait(VEC, t_x)
                        bld.wait(VEC, free_mm[sl])
                        tv = bld.inc(VEC.tensor_copy(xbf[sl][:], xin[sl][:]), "tV")
                        bld.wait(SCA, t_x)
                        bld.wait(SCA, free_mm[sl])
                        tc = bld.inc(SCA.activation(xsq[sl][:], xin[sl][:], AF.Square), "tC")
                        free_in[sl] = [tv, tc]
                        bld.wait(TEN, tv)
                        bld.wait(TEN, tc)
                        last = None
                        for tt in range(2):
                            TEN.matmul(psx[0:1, tt * 512:(tt + 1) * 512], ones_b[:],
                                       xbf[sl][:, tt * 512:(tt + 1) * 512],
                                       start=(hc == 0), stop=(hc == HC - 1))
                            last = TEN.matmul(psq[0:1, tt * 512:(tt + 1) * 512],
                                              ones_b[:], xsq[sl][:, tt * 512:(tt + 1) * 512],
                                              start=(hc == 0), stop=(hc == HC - 1))
                        free_mm[sl] = bld.inc(last, "tT")
                        bld.wait(TEN, t_w)
                        for tt in range(2):
                            last = TEN.matmul(pq[0:1, tt * 512:(tt + 1) * 512], nones_b[:],
                                              cswb[wl][:, tt * 512:(tt + 1) * 512],
                                              start=(hc == 0), stop=(hc == HC - 1))
                        freew[wl] = bld.inc(last, "tT")
                        bld.wait(TEN, t_wk)
                        freekw[wl] = bld.inc(
                            TEN.matmul(pkv[0:1, :], nones_b[:], ckwb[wl][:, 0:KVR],
                                       start=(hc == 0), stop=(hc == HC - 1)), "tT")
                        if hc == 0:
                            bld.wait(TEN, t_wkrA)
                        t_ka = bld.inc(
                            TEN.matmul(pkab[0:1, :], nones_b[:],
                                       wkrA_sb[:, hc * 64:(hc + 1) * 64],
                                       start=(hc == 0), stop=(hc == HC - 1)), "tT")
                    bld.wait(VEC, freew[1])
                    bld.wait(VEC, freekw[1])
                    bld.wait(VEC, t_ka)
                    VEC.tensor_copy(ncs_q[:], pq[0:1, :])
                    VEC.tensor_copy(ncs_kv[:], pkv[0:1, :])
                    tv_a = bld.inc(VEC.tensor_copy(ncs_kA[:], pkab[0:1, :]), "tV")
                    bld.wait(TEN, tv_a)
                    bld.wait(TEN, t_wkrB)
                    last = None
                    for hc in range(HC):
                        last = TEN.matmul(pkab[0:1, :], nones_b[:],
                                          wkrB_sb[:, hc * 64:(hc + 1) * 64],
                                          start=(hc == 0), stop=(hc == HC - 1))
                    t_kb = bld.inc(last, "tT")
                    bld.wait(VEC, t_kb)
                    tv_b = bld.inc(VEC.tensor_copy(ncs_kB[:], pkab[0:1, :]), "tV")
                    bld.wait(VEC, free_mm[0])
                    bld.wait(VEC, free_mm[1])
                    bld.wait(VEC, free_mm[2])
                    W = NOWN
                    VEC.tensor_scalar(rows[0:1, 0:W], psx[0:1, :], 1.0 / H, None, ALU.mult)
                    VEC.tensor_scalar(rows[0:1, W:2 * W], psq[0:1, :], 1.0 / H, None, ALU.mult)
                    VEC.scalar_tensor_tensor(psx[0:1, :], rows[0:1, 0:W], -1.0,
                                             rows[0:1, 0:W], ALU.mult, ALU.mult)
                    VEC.tensor_tensor(rows[0:1, W:2 * W], rows[0:1, W:2 * W],
                                      psx[0:1, :], ALU.add)
                    tv = bld.inc(VEC.tensor_copy(mo_bf[0:1, :], rows[0:1, 0:W]), "tV")
                    bld.wait(TEN, tv)
                    bld.wait(TEN, tv_b)
                W = NOWN
                bld.wait(SCA, tv)
                SCA.activation(rows[0:1, W:2 * W], rows[0:1, W:2 * W], AF.Ln, bias=eps_ln[:])
                tc = bld.inc(SCA.activation(rows[0:1, W:2 * W], rows[0:1, W:2 * W],
                                            AF.Exp, scale=-0.5), "tC")
                with nc.psum_tensor("lnopsb", [128, NOWN], F32) as psb:
                    bld.wait(TEN, tc)
                    last = None
                    for tt in range(NOWN // 512):
                        last = TEN.matmul(psb[:, tt * 512:(tt + 1) * 512], ones_fr[:],
                                          rows[0:1, W + tt * 512: W + (tt + 1) * 512],
                                          start=True, stop=True)
                    tmm = bld.inc(last, "tT")
                    bld.wait(VEC, tmm)
                    tv = bld.inc(VEC.tensor_copy(ro_b[:, :], psb[:, :]), "tV")
                    bld.wait(TEN, tv)
            tap("ncs_q", ncs_q[:], (1, QR))

            dma_fence()
            with ExitStack() as st_:
                S_ = st_.enter_context
                xqc_sb = S_(nc.sbuf_tensor("xqc", [128, QRC * 1024], BF16))
                xin = [S_(nc.sbuf_tensor(f"qxin{i}", [128, 512], F32)) for i in range(4)]
                xbf = [S_(nc.sbuf_tensor(f"qxbf{i}", [128, 512], BF16)) for i in range(4)]
                wbuf = [S_(nc.sbuf_tensor(f"qw{i}", [128, QR], BF16)) for i in range(3)]
                sqb = [S_(nc.sbuf_tensor(f"qsq{i}", [128, 1024], BF16)) for i in range(2)]
                rowq = S_(nc.sbuf_tensor("rowq", [1, NOWN], F32))
                rq_b = S_(nc.sbuf_tensor("rq_b", [128, NOWN], F32))
                with ExitStack() as ps_:
                    pxq = ps_.enter_context(nc.psum_tensor("pxq", [128, QRC * 512], F32))
                    freew = [None] * 3
                    freex = [None] * 4
                    freeb = [None] * 4
                    for th in range(2):
                        for hc in range(HC):
                            i = th * HC + hc
                            sl = i % 4
                            wl = i % 3
                            t_x = dma_s(f"qx{sl}", xin[sl][:],
                                        xownT[hc * 128:(hc + 1) * 128, th * 512:(th + 1) * 512],
                                        free_tok=freex[sl])
                            t_w = dma_g(f"qw{wl}", wbuf[wl][:], w_cq[hc * 128:(hc + 1) * 128, :],
                                        free_tok=freew[wl])
                            bld.wait(VEC, t_x)
                            bld.wait(VEC, freeb[sl])
                            tv = bld.inc(VEC.tensor_copy(xbf[sl][:], xin[sl][:]), "tV")
                            freex[sl] = tv
                            bld.wait(TEN, tv)
                            bld.wait(TEN, t_w)
                            last = None
                            for qrc in range(QRC):
                                last = TEN.matmul(pxq[:, qrc * 512:(qrc + 1) * 512],
                                                  wbuf[wl][:, qrc * 128:(qrc + 1) * 128], xbf[sl][:],
                                                  start=(hc == 0), stop=False)
                            t_mm = bld.inc(last, "tT")
                            freew[wl] = t_mm
                            freeb[sl] = t_mm
                        last = None
                        for qrc in range(QRC):
                            last = TEN.matmul(pxq[:, qrc * 512:(qrc + 1) * 512],
                                              ncs_q[0:1, qrc * 128:(qrc + 1) * 128],
                                              mo_bf[0:1, th * 512:(th + 1) * 512],
                                              start=False, stop=True)
                        tmm = bld.inc(last, "tT")
                        bld.wait(VEC, tmm)
                        last = None
                        for qrc in range(QRC):
                            last = VEC.tensor_tensor(
                                xqc_sb[:, qrc * 1024 + th * 512: qrc * 1024 + (th + 1) * 512],
                                pxq[:, qrc * 512:(qrc + 1) * 512],
                                ro_b[:, th * 512:(th + 1) * 512], ALU.mult)
                        tv = bld.inc(last, "tV")
                        bld.wait(TEN, tv)
                with ExitStack() as ps_:
                    pss = ps_.enter_context(nc.psum_tensor("pssq", [1, NOWN], F32))
                    frees = [bld.tok("tV"), bld.tok("tV")]
                    for qrc in range(QRC):
                        sl = qrc % 2
                        bld.wait(SCA, frees[sl])
                        tc = bld.inc(SCA.activation(sqb[sl][:],
                                                    xqc_sb[:, qrc * 1024:(qrc + 1) * 1024],
                                                    AF.Square), "tC")
                        bld.wait(TEN, tc)
                        last = None
                        for tt in range(2):
                            last = TEN.matmul(pss[0:1, tt * 512:(tt + 1) * 512], ones_b[:],
                                              sqb[sl][:, tt * 512:(tt + 1) * 512],
                                              start=(qrc == 0), stop=(qrc == QRC - 1))
                        frees[sl] = bld.inc(last, "tT")
                    bld.wait(VEC, frees[(QRC - 1) % 2])
                    tv = bld.inc(VEC.tensor_scalar(rowq[0:1, 0:NOWN], pss[0:1, :],
                                                   1.0 / QR, None, ALU.mult), "tV")
                    bld.wait(SCA, tv)
                    SCA.activation(rowq[0:1, 0:NOWN], rowq[0:1, 0:NOWN], AF.Ln,
                                   bias=eps_rms[:])
                    tc = bld.inc(SCA.activation(rowq[0:1, 0:NOWN], rowq[0:1, 0:NOWN],
                                                AF.Exp, scale=-0.5), "tC")
                    bld.wait(TEN, tc)
                with nc.psum_tensor("psbq", [128, NOWN], F32) as psb:
                    last = None
                    for tt in range(2):
                        last = TEN.matmul(psb[:, tt * 512:(tt + 1) * 512], ones_fr[:],
                                          rowq[0:1, tt * 512:(tt + 1) * 512], start=True, stop=True)
                    tmm = bld.inc(last, "tT")
                    bld.wait(VEC, tmm)
                    bld.wait(VEC, t_const)
                    VEC.tensor_copy(rq_b[:], psb[:])
                    last = None
                    for qrc in range(QRC):
                        last = VEC.scalar_tensor_tensor(normq_sb[:, qrc * 1024:(qrc + 1) * 1024],
                                                        xqc_sb[:, qrc * 1024:(qrc + 1) * 1024],
                                                        qs_sb[:, qrc:qrc + 1], rq_b[:],
                                                        ALU.mult, ALU.mult)
                    tv = bld.inc(last, "tV")
                    bld.wait(TEN, tv)
            tap("norm_q", normq_sb[:], (128, QRC * 1024))

            dma_fence()
            with ExitStack() as st_:
                S_ = st_.enter_context
                cck_sb = S_(nc.sbuf_tensor("cckS", [64, T], BF16))
                ssk_sb = S_(nc.sbuf_tensor("sskS", [64, T], BF16))
                xkvc_sb = S_(nc.sbuf_tensor("xkvc", [128, KVC * T], BF16))
                xin = [S_(nc.sbuf_tensor(f"kxin{i}", [128, 512], F32)) for i in range(4)]
                xbf = [S_(nc.sbuf_tensor(f"kxbf{i}", [128, 512], BF16)) for i in range(4)]
                wbuf = [S_(nc.sbuf_tensor(f"kw{i}", [128, KVR], BF16)) for i in range(4)]
                tmpa = S_(nc.sbuf_tensor("ktmpa", [64, 512], F32))
                tmpb = S_(nc.sbuf_tensor("ktmpb", [64, 512], F32))
                sqb = [S_(nc.sbuf_tensor(f"ksq{i}", [128, 1024], BF16)) for i in range(2)]
                rowk = S_(nc.sbuf_tensor("rowk", [1, T], F32))
                rkv_b = S_(nc.sbuf_tensor("rkv_b", [128, T], F32))
                dma_g("tabk", cck_sb[:], cckD)
                t_tabk = dma_g("tabk", ssk_sb[:], sskD)
                with ExitStack() as ps_:
                    pkv = ps_.enter_context(nc.psum_tensor("pkv", [128, KVC * 512], F32))
                    pka = ps_.enter_context(nc.psum_tensor("pka", [64, 512], F32))
                    pkb = ps_.enter_context(nc.psum_tensor("pkb", [64, 512], F32))
                    freew = [None] * 4
                    freex = [None] * 4
                    freeb = [None] * 4
                    for tq in range(4):
                        for hc in range(HC):
                            i = tq * HC + hc
                            sl = i % 4
                            t_x = dma_s(f"kx{sl}", xin[sl][:],
                                        xT[hc * 128:(hc + 1) * 128, tq * 512:(tq + 1) * 512],
                                        free_tok=freex[sl])
                            t_w = dma_g(f"kw{sl}", wbuf[sl][:], w_ckv[hc * 128:(hc + 1) * 128, :],
                                        free_tok=freew[sl])
                            bld.wait(VEC, t_x)
                            bld.wait(VEC, freeb[sl])
                            tv = bld.inc(VEC.tensor_copy(xbf[sl][:], xin[sl][:]), "tV")
                            freex[sl] = tv
                            bld.wait(TEN, tv)
                            bld.wait(TEN, t_w)
                            for kvc in range(KVC):
                                TEN.matmul(pkv[:, kvc * 512:(kvc + 1) * 512],
                                           wbuf[sl][:, kvc * 128:(kvc + 1) * 128], xbf[sl][:],
                                           start=(hc == 0), stop=False)
                            TEN.matmul(pka[:, :], wkrA_sb[:, hc * 64:(hc + 1) * 64], xbf[sl][:],
                                       start=(hc == 0), stop=False)
                            last = TEN.matmul(pkb[:, :], wkrB_sb[:, hc * 64:(hc + 1) * 64],
                                              xbf[sl][:], start=(hc == 0), stop=False)
                            t_mm = bld.inc(last, "tT")
                            freew[sl] = t_mm
                            freeb[sl] = t_mm
                        mslice = m_bf[0:1, tq * 512:(tq + 1) * 512]
                        for kvc in range(KVC):
                            TEN.matmul(pkv[:, kvc * 512:(kvc + 1) * 512],
                                       ncs_kv[0:1, kvc * 128:(kvc + 1) * 128], mslice,
                                       start=False, stop=True)
                        TEN.matmul(pka[:, :], ncs_kA[0:1, :], mslice, start=False, stop=True)
                        last = TEN.matmul(pkb[:, :], ncs_kB[0:1, :], mslice, start=False, stop=True)
                        tmm = bld.inc(last, "tT")
                        bld.wait(VEC, tmm)
                        if tq == 0:
                            bld.wait(VEC, t_tabk)
                        rsl = r_b[:, tq * 512:(tq + 1) * 512]
                        for kvc in range(KVC):
                            VEC.tensor_tensor(xkvc_sb[:, kvc * T + tq * 512: kvc * T + (tq + 1) * 512],
                                              pkv[:, kvc * 512:(kvc + 1) * 512], rsl, ALU.mult)
                        VEC.tensor_tensor(tmpa[:], pka[:, :], rsl[0:64, :], ALU.mult)
                        VEC.tensor_tensor(tmpa[:], tmpa[:], cck_sb[:, tq * 512:(tq + 1) * 512],
                                          ALU.mult)
                        VEC.tensor_tensor(tmpb[:], pkb[:, :], rsl[0:64, :], ALU.mult)
                        VEC.tensor_tensor(tmpb[:], tmpb[:], ssk_sb[:, tq * 512:(tq + 1) * 512],
                                          ALU.mult)
                        last = VEC.tensor_tensor(krope_sb[:, tq * 512:(tq + 1) * 512],
                                                 tmpa[:], tmpb[:], ALU.add)
                        tv = bld.inc(last, "tV")
                        bld.wait(TEN, tv)
                with ExitStack() as ps_:
                    pss = ps_.enter_context(nc.psum_tensor("pssk", [1, T], F32))
                    frees = [bld.tok("tV"), bld.tok("tV")]
                    for kvc in range(KVC):
                        for th2 in range(2):
                            sl = (2 * kvc + th2) % 2
                            bld.wait(SCA, frees[sl])
                            tc = bld.inc(SCA.activation(
                                sqb[sl][:],
                                xkvc_sb[:, kvc * T + th2 * 1024: kvc * T + (th2 + 1) * 1024],
                                AF.Square), "tC")
                            bld.wait(TEN, tc)
                            last = None
                            for tt in range(2):
                                last = TEN.matmul(
                                    pss[0:1, th2 * 1024 + tt * 512: th2 * 1024 + (tt + 1) * 512],
                                    ones_b[:], sqb[sl][:, tt * 512:(tt + 1) * 512],
                                    start=(kvc == 0), stop=(kvc == KVC - 1))
                            frees[sl] = bld.inc(last, "tT")
                    bld.wait(VEC, frees[0])
                    bld.wait(VEC, frees[1])
                    tv = bld.inc(VEC.tensor_scalar(rowk[0:1, 0:T], pss[0:1, :], 1.0 / KVR,
                                                   None, ALU.mult), "tV")
                    bld.wait(SCA, tv)
                    SCA.activation(rowk[0:1, 0:T], rowk[0:1, 0:T], AF.Ln, bias=eps_rms[:])
                    tc = bld.inc(SCA.activation(rowk[0:1, 0:T], rowk[0:1, 0:T],
                                                AF.Exp, scale=-0.5), "tC")
                    bld.wait(TEN, tc)
                with nc.psum_tensor("psbk", [128, T], F32) as psb:
                    last = None
                    for tt in range(4):
                        last = TEN.matmul(psb[:, tt * 512:(tt + 1) * 512], ones_fr[:],
                                          rowk[0:1, tt * 512:(tt + 1) * 512], start=True, stop=True)
                    tmm = bld.inc(last, "tT")
                    bld.wait(VEC, tmm)
                    VEC.tensor_copy(rkv_b[:], psb[:])
                    last = None
                    for kvc in range(KVC):
                        last = VEC.scalar_tensor_tensor(normkv_sb[:, kvc * T:(kvc + 1) * T],
                                                        xkvc_sb[:, kvc * T:(kvc + 1) * T],
                                                        kvs_sb[:, kvc:kvc + 1], rkv_b[:],
                                                        ALU.mult, ALU.mult)
                    tv = bld.inc(last, "tV")
                    bld.wait(TEN, tv)
            tap("norm_kv", normkv_sb[:], (128, KVC * T))
            tap("krope", krope_sb[0:64, :], (64, T))
            lvW.close()

            dma_fence()
            lvAA = ExitStack()
            aall_sb = lvAA.enter_context(
                nc.sbuf_tensor("aall", [128, NH * 1024], BF16, side="right"))
            with ExitStack() as st_:
                S_ = st_.enter_context
                K2 = [S_(nc.sbuf_tensor(f"K{i}", [128, T], BF16)) for i in range(2)]
                V2 = [S_(nc.sbuf_tensor(f"V{i}", [128, T], BF16)) for i in range(2)]
                Vt_sb = S_(nc.sbuf_tensor("Vt", [128, T], BF16))
                Qn2 = [S_(nc.sbuf_tensor(f"Qn{i}", [128, NOWN], BF16)) for i in range(2)]
                Qr2 = [S_(nc.sbuf_tensor(f"Qr{i}", [128, NOWN], BF16)) for i in range(2)]
                Qro2 = [S_(nc.sbuf_tensor(f"Qro{i}", [64, NOWN], BF16)) for i in range(2)]
                P2 = [S_(nc.sbuf_tensor("P0", [128, T - 256], BF16)),
                      S_(nc.sbuf_tensor("P1", [128, T], BF16))]
                PT_sb = S_(nc.sbuf_tensor("PT", [128, 16 * 512], BF16))
                negm_sb = S_(nc.sbuf_tensor("negm", [128, NSLOT], F32))
                mtmp_sb = S_(nc.sbuf_tensor("mtmp", [128, 4], F32))
                l_sb = S_(nc.sbuf_tensor("l", [128, NSLOT], F32))
                l2_sb = S_(nc.sbuf_tensor("l2", [128, NSLOT], F32))
                linv_sb = S_(nc.sbuf_tensor("linv", [128, NSLOT], F32))
                tmpA = S_(nc.sbuf_tensor("qtmpA", [128, NOWN], BF16))
                wqn_b = [S_(nc.sbuf_tensor(f"wqn{i}", [128, QR], BF16)) for i in range(2)]
                wqra_b = S_(nc.sbuf_tensor("wqra", [128, QR], BF16))
                wqrb_b = S_(nc.sbuf_tensor("wqrb", [128, QR], BF16))
                wdk_b = [S_(nc.sbuf_tensor(f"wdk{i}", [128, KVR], BF16)) for i in range(2)]
                wdv_b = [S_(nc.sbuf_tensor(f"wdv{i}", [128, KVR], BF16)) for i in range(2)]

                S_ps = S_(nc.psum_tensor("S_ps", [128, T], F32))
                A_ps = S_(nc.psum_tensor("A_ps", [128, 512], F32))
                PT_ps = S_(nc.psum_tensor("PT_ps", [128, 1024], BF16))
                pprod = S_(nc.psum_tensor("pprod", [128, 1024], F32))

                bld.wait(TEN, t_const)
                bld.wait(VEC, t_ccq)
                bld.wait(VEC, t_smask)

                wq_free = [None, None]
                wqa_free = None
                wqb_free = None
                wk_free = [None, None]
                wv_free = [None, None]
                t_wqn = [None, None]
                t_wqa = [None]
                t_wqb = [None]
                t_wdk = [None, None]
                t_wdv = [None, None]

                def issue_head_loads(h):
                    sl = h % 2
                    t_wqn[sl] = dma_g(f"wqn{sl}", wqn_b[sl][:], wqnD[h], free_tok=wq_free[sl])
                    t_wdk[sl] = dma_g(f"wdk{sl}", wdk_b[sl][:], wdkD[h], free_tok=wk_free[sl])
                    t_wdv[sl] = dma_g(f"wdv{sl}", wdv_b[sl][:], wdvD[h], free_tok=wv_free[sl])
                    if h % 2 == 0:
                        t_wqa[0] = dma_g("wqra", wqra_b[:], wqrAD[h // 2], free_tok=wqa_free)
                        t_wqb[0] = dma_g("wqrb", wqrb_b[:], wqrBD[h // 2], free_tok=wqb_free)

                issue_head_loads(0)
                issue_head_loads(1)

                kv_half_tok = [None, None]
                tok_qn = [None, None]
                tok_qr = [None, None]
                tok_qro = [None, None]
                tok_k = [None, None]
                tok_v = [None, None]
                t_ptev = None
                prod_pend = []

                def prod_sub(h, c, part):
                    nonlocal t_ptev, wqa_free, wqb_free
                    p = h % 2
                    if c == 0:
                        tt = part
                        bld.wait(TEN, t_wqn[p])
                        if part == 0:
                            bld.wait(TEN, tok_qn[p])
                            bld.wait(TEN, kv_half_tok[0])
                            bld.wait(TEN, kv_half_tok[1])
                        last = None
                        for qrc in range(QRC):
                            last = TEN.matmul(
                                pprod[:, tt * 512:(tt + 1) * 512],
                                wqn_b[p][:, qrc * 128:(qrc + 1) * 128],
                                normq_sb[:, qrc * 1024 + tt * 512: qrc * 1024 + (tt + 1) * 512],
                                start=(qrc == 0), stop=(qrc == QRC - 1))
                        if part == 1:
                            wq_free[p] = bld.inc(last, "tT")
                            bld.wait(SCA, wq_free[p])
                            tok_qn[p] = bld.inc(SCA.activation(Qn2[p][:], pprod[:, 0:1024],
                                                               AF.Copy, scale=SCALE), "tC")
                    elif c == 1:
                        pp2 = (h // 2) % 2
                        tt = part
                        bld.wait(TEN, t_wqa[0])
                        if part == 0:
                            bld.wait(TEN, tok_qn[p])
                        last = None
                        for qrc in range(QRC):
                            last = TEN.matmul(
                                pprod[:, tt * 512:(tt + 1) * 512],
                                wqra_b[:, qrc * 128:(qrc + 1) * 128],
                                normq_sb[:, qrc * 1024 + tt * 512: qrc * 1024 + (tt + 1) * 512],
                                start=(qrc == 0), stop=(qrc == QRC - 1))
                        if part == 1:
                            wqa_free = bld.inc(last, "tT")
                            bld.wait(VEC, wqa_free)
                            tok_qr[pp2] = bld.inc(VEC.tensor_tensor(tmpA[:], pprod[:, 0:1024],
                                                                    ccq_sb[:], ALU.mult), "tV")
                            bld.wait(TEN, tok_qr[pp2])
                    elif c == 2:
                        pp2 = (h // 2) % 2
                        tt = part
                        bld.wait(TEN, t_wqb[0])
                        last = None
                        for qrc in range(QRC):
                            last = TEN.matmul(
                                pprod[:, tt * 512:(tt + 1) * 512],
                                wqrb_b[:, qrc * 128:(qrc + 1) * 128],
                                normq_sb[:, qrc * 1024 + tt * 512: qrc * 1024 + (tt + 1) * 512],
                                start=(qrc == 0), stop=(qrc == QRC - 1))
                        if part == 1:
                            wqb_free = bld.inc(last, "tT")
                            bld.wait(VEC, wqb_free)
                            bld.wait(VEC, tok_qro[pp2])
                            VEC.tensor_tensor(Qr2[pp2][:], pprod[:, 0:1024], ssq_sb[:], ALU.mult)
                            tok_qr[pp2] = bld.inc(VEC.tensor_tensor(Qr2[pp2][:], tmpA[:],
                                                                    Qr2[pp2][:], ALU.add), "tV")
                            bld.wait(TEN, tok_qr[pp2])
                            tok_qro[pp2] = dma_g(f"qro{pp2}", Qro2[pp2][:], Qr2[pp2][64:128, :],
                                                 free_tok=[tok_qr[pp2], bld.tok("tT")])
                    elif c in (3, 4):
                        tt = 2 * (c - 3) + part
                        bld.wait(TEN, t_wdk[p])
                        if tt == 0:
                            bld.wait(TEN, tok_qn[p])
                        pp = pprod[:, (tt % 2) * 512:((tt % 2) + 1) * 512]
                        bld.wait(TEN, kv_half_tok[tt % 2])
                        last = None
                        for kvc in range(KVC):
                            last = TEN.matmul(pp, wdk_b[p][:, kvc * 128:(kvc + 1) * 128],
                                              normkv_sb[:, kvc * T + tt * 512: kvc * T + (tt + 1) * 512],
                                              start=(kvc == 0), stop=(kvc == KVC - 1))
                        tmm = bld.inc(last, "tT")
                        bld.wait(VEC, tmm)
                        kv_half_tok[tt % 2] = bld.inc(
                            VEC.tensor_copy(K2[p][:, tt * 512:(tt + 1) * 512], pp), "tV")
                        if tt == 3:
                            tok_k[p] = kv_half_tok[1]
                            wk_free[p] = kv_half_tok[1]
                    elif c in (5, 6):
                        tt = 2 * (c - 5) + part
                        bld.wait(TEN, t_wdv[p])
                        pp = pprod[:, (tt % 2) * 512:((tt % 2) + 1) * 512]
                        bld.wait(TEN, kv_half_tok[tt % 2])
                        last = None
                        for kvc in range(KVC):
                            last = TEN.matmul(pp, wdv_b[p][:, kvc * 128:(kvc + 1) * 128],
                                              normkv_sb[:, kvc * T + tt * 512: kvc * T + (tt + 1) * 512],
                                              start=(kvc == 0), stop=(kvc == KVC - 1))
                        tmm = bld.inc(last, "tT")
                        bld.wait(VEC, tmm)
                        kv_half_tok[tt % 2] = bld.inc(
                            VEC.tensor_copy(Vt_sb[:, tt * 512:(tt + 1) * 512], pp), "tV")
                        if tt == 3:
                            wv_free[p] = kv_half_tok[1]
                    elif c == 7:
                        g8 = part
                        bld.wait(TEN, t_ptev)
                        if g8 == 0:
                            bld.wait(TEN, kv_half_tok[0])
                            bld.wait(TEN, kv_half_tok[1])
                        last = None
                        for j8 in range(8):
                            kt = 8 * g8 + j8
                            last = TEN.transpose(PT_ps[:, j8 * 128:(j8 + 1) * 128],
                                                 Vt_sb[:, kt * 128:(kt + 1) * 128], ident_sb[:])
                        tmm = bld.inc(last, "tT")
                        if g8 % 2 == 0:
                            bld.wait(VEC, tmm)
                            t_ptev = bld.inc(
                                VEC.tensor_copy(V2[p][:, g8 * 1024:(g8 + 1) * 1024],
                                                PT_ps[:, :]), "tV")
                        else:
                            bld.wait(SCA, tmm)
                            t_ptev = bld.inc(
                                SCA.activation(V2[p][:, g8 * 1024:(g8 + 1) * 1024],
                                               PT_ps[:, :], AF.Copy), "tC")
                            tok_v[p] = t_ptev

                prod_q = []
                prod_tgt = [0]

                def fill_queue(h):
                    for c in range(8):
                        if c in (1, 2) and h % 2 != 0:
                            continue
                        for part in range(2):
                            prod_q.append((h, c, part))

                def set_slot_target(s):
                    n0 = 16
                    prod_tgt[0] = max(0, n0 * (7 - s) // 8)

                def pop_prod():
                    if prod_q and len(prod_q) > prod_tgt[0]:
                        prod_sub(*prod_q.pop(0))

                def drain_prod():
                    while prod_q:
                        prod_sub(*prod_q.pop(0))

                fill_queue(0)
                drain_prod()

                t_exp_prev = None
                t_exp_c0 = None
                pt_evac_tok = [[] for _ in range(NSLOT)]
                t_psc = [None] * NSLOT

                def s_mms(h, s):
                    nonlocal t_exp_prev, t_exp_c0
                    p = h % 2
                    kext = 256 * (s + 1)
                    bld.wait(TEN, t_exp_c0)
                    if s == 0:
                        bld.wait(TEN, tok_qn[p])
                        bld.wait(TEN, tok_k[p])
                        pp2 = (h // 2) % 2
                        if h % 2 == 0:
                            bld.wait(TEN, tok_qr[pp2])
                        else:
                            bld.wait(TEN, tok_qro[pp2])
                    pp2 = (h // 2) % 2
                    qr_l = Qr2[pp2] if h % 2 == 0 else Qro2[pp2]
                    nkc = (kext + 511) // 512
                    for kc in range(nkc):
                        if kc == 1:
                            bld.wait(TEN, t_exp_prev)
                        n = min(512, kext - kc * 512)
                        TEN.matmul(S_ps[:, kc * 512: kc * 512 + n],
                                   Qn2[p][:, s * 128:(s + 1) * 128],
                                   K2[p][:, kc * 512: kc * 512 + n],
                                   start=True, stop=False)
                        t_kc = bld.inc(TEN.matmul(S_ps[:, kc * 512: kc * 512 + n],
                                                  qr_l[0:64, s * 128:(s + 1) * 128],
                                                  krope_sb[:, kc * 512: kc * 512 + n],
                                                  start=False, stop=True), "tT")
                        bld.wait(VEC, t_kc)
                        VEC.reduce_max(mtmp_sb[:, kc:kc + 1],
                                       S_ps[:, kc * 512: kc * 512 + n], axis=AX.X)
                    t_max = bld.inc(VEC.reduce_max(negm_sb[:, s:s + 1], mtmp_sb[:, 0:nkc],
                                                   axis=AX.X, negate=True), "tV")
                    t_msk = bld.inc(VEC.tensor_tensor(S_ps[:, kext - 256:kext],
                                                      S_ps[:, kext - 256:kext],
                                                      smask_sb[:], ALU.add), "tV")
                    if kext <= 512:
                        bld.wait(SCA, t_msk)
                        t_exp_prev = bld.inc(
                            SCA.activation(P2[s % 2][:, 0:kext], S_ps[:, 0:kext], AF.Exp,
                                           bias=negm_sb[:, s:s + 1],
                                           accum_out=l_sb[:, s:s + 1]), "tC")
                        t_exp_c0 = t_exp_prev
                    else:
                        bld.wait(SCA, t_max)
                        t_exp_c0 = bld.inc(
                            SCA.activation(P2[s % 2][:, 0:512], S_ps[:, 0:512], AF.Exp,
                                           bias=negm_sb[:, s:s + 1],
                                           accum_out=l_sb[:, s:s + 1]), "tC")
                        bld.wait(SCA, t_msk)
                        t_exp_prev = bld.inc(
                            SCA.activation(P2[s % 2][:, 512:kext], S_ps[:, 512:kext], AF.Exp,
                                           bias=negm_sb[:, s:s + 1],
                                           accum_out=l2_sb[:, s:s + 1]), "tC")

                def s_finish(h, s):
                    kext = 256 * (s + 1)
                    tl = t_exp_prev
                    if kext > 512:
                        tl = bld.inc(SCA.activation(l_sb[:, s:s + 1], l_sb[:, s:s + 1],
                                                    AF.Identity,
                                                    bias=l2_sb[:, s:s + 1]), "tC")
                    bld.wait(VEC, tl)
                    VEC.reciprocal(linv_sb[:, s:s + 1], l_sb[:, s:s + 1])
                    VEC.engine_nop()
                    tv = bld.inc(VEC.engine_nop(), "tV")
                    bld.wait(SCA, tv)
                    t_psc[s] = bld.inc(
                        SCA.activation(P2[s % 2][:, 0:kext], P2[s % 2][:, 0:kext], AF.Copy,
                                       scale=linv_sb[:, s:s + 1]), "tC")

                def p_transposes(h, s, mid=None):
                    nonlocal t_ptev
                    kext = 256 * (s + 1)
                    nkt = kext // 128
                    si = s % 4
                    pt_evac_tok[s] = []
                    bld.wait(TEN, t_psc[s])
                    for a in range((nkt + 7) // 8):
                        if a >= 1 and mid is not None:
                            mid()
                        bld.wait(TEN, t_ptev)
                        nb = min(8, nkt - 8 * a)
                        last = None
                        for bji in range(nb):
                            kt = 8 * a + bji
                            last = TEN.transpose(PT_ps[:, bji * 128:(bji + 1) * 128],
                                                 P2[s % 2][:, kt * 128:(kt + 1) * 128],
                                                 ident_sb[:])
                        tmm = bld.inc(last, "tT")
                        dst = PT_sb[:].rearrange("p (k q) -> p k q", q=512)[
                            :, 8 * a:8 * a + nb, si * 128:(si + 1) * 128]
                        srcp = PT_ps[:].rearrange("p (k q) -> p k q", q=128)[:, 0:nb, :]
                        if a % 2 == 0:
                            bld.wait(VEC, tmm)
                            t_ptev = bld.inc(VEC.tensor_copy(dst, srcp), "tV")
                        else:
                            bld.wait(SCA, tmm)
                            t_ptev = bld.inc(SCA.activation(dst, srcp, AF.Copy), "tC")
                        pt_evac_tok[s].append(t_ptev)

                def av_group(h, grp):
                    p = h % 2
                    nkt_g = 8 if grp == 0 else 16
                    bld.wait(TEN, tok_v[p])
                    for s in range(grp * 4, grp * 4 + 4):
                        bld.wait(TEN, pt_evac_tok[s])
                    last = None
                    for kt in range(nkt_g):
                        st0 = (kt // 2) * 128 if grp == 0 else max(0, kt // 2 - 4) * 128
                        last = TEN.matmul(A_ps[:, st0:512],
                                          V2[p][:, kt * 128:(kt + 1) * 128],
                                          PT_sb[:, kt * 512 + st0: kt * 512 + 512],
                                          start=(kt == 0), stop=(kt == nkt_g - 1))
                    t_av = bld.inc(last, "tT")
                    bld.wait(VEC, t_av)
                    t_aev = bld.inc(VEC.tensor_copy(
                        aall_sb[:, h * 1024 + grp * 512: h * 1024 + (grp + 1) * 512],
                        A_ps[:, :]), "tV")
                    bld.wait(TEN, t_aev)

                for h in range(NH):
                    if h + 2 < NH:
                        issue_head_loads(h + 2)
                    if h + 1 < NH:
                        fill_queue(h + 1)
                    for s in range(NSLOT):
                        set_slot_target(s)
                        s_mms(h, s)
                        pop_prod()
                        if s >= 1:
                            p_transposes(h, s - 1, pop_prod)
                        elif h >= 1:
                            p_transposes(h - 1, 7, pop_prod)
                        pop_prod()
                        if s == 0 and h >= 1:
                            av_group(h - 1, 1)
                        if s == 4:
                            av_group(h, 0)
                        s_finish(h, s)
                    drain_prod()
                p_transposes(NH - 1, 7)
                av_group(NH - 1, 1)
                tap("A_all", aall_sb[:], (128, NH * 1024))
                tap("K_last", K2[1][:], (128, T))
                tap("V_last", V2[1][:], (128, T))
                tap("Qn_last", Qn2[1][:], (128, NOWN))
                tap("P_last", P2[1][:], (128, T))
                tap("negm", negm_sb[:], (128, NSLOT))
                tap("l", l_sb[:], (128, NSLOT))
            cbar()
            dma_fence()
            lvP2.close()
            xf_sb = A_(nc.sbuf_tensor("xf", [128, HC * 1024], BF16))
            xf_tok = []
            pre_norm_fence = [None, None, None]

            dma_fence()
            with ExitStack() as st_:
                S_ = st_.enter_context
                wo_b = [S_(nc.sbuf_tensor(f"wo{i}", [128, NH * 128], BF16)) for i in range(2)]
                xo_b = [S_(nc.sbuf_tensor(f"xo{i}", [128, 1024], F32)) for i in range(2)]
                sqb = [S_(nc.sbuf_tensor(f"fsq{i}", [128, 1024], BF16)) for i in range(2)]
                xbb = [S_(nc.sbuf_tensor(f"fxb{i}", [128, 1024], BF16)) for i in range(2)]
                p45 = st_.enter_context(ExitStack())
                py = p45.enter_context(nc.psum_tensor("py", [128, 2 * 1024], F32))
                psx = p45.enter_context(nc.psum_tensor("fpsx", [1, 1024], F32))
                psq = p45.enter_context(nc.psum_tensor("fpsq", [1, 1024], F32))
                wfree = [None, None]
                xfree = [None, None]
                evtok = [None, None]
                stfree = [None, None]
                for oc in range(HC):
                    sl = oc % 2
                    t_w = dma_g(f"wo{sl}", wo_b[sl][:], woD[oc], free_tok=wfree[sl])
                    t_x = dma_s(f"xo{sl}", xo_b[sl][:], xownT[oc * 128:(oc + 1) * 128, :],
                                free_tok=xfree[sl])
                    pp = py[:, sl * 1024:(sl + 1) * 1024]
                    bld.wait(TEN, t_w)
                    bld.wait(TEN, evtok[sl])
                    last = None
                    for vdc in range(NH):
                        for tt in range(2):
                            last = TEN.matmul(
                                pp[:, tt * 512:(tt + 1) * 512],
                                wo_b[sl][:, vdc * 128:(vdc + 1) * 128],
                                aall_sb[:, vdc * 1024 + tt * 512: vdc * 1024 + (tt + 1) * 512],
                                start=(vdc == 0), stop=(vdc == NH - 1))
                    tmm = bld.inc(last, "tT")
                    wfree[sl] = tmm
                    bld.wait(VEC, tmm)
                    bld.wait(VEC, t_x)
                    bld.wait(VEC, stfree[sl])
                    tv = bld.inc(VEC.tensor_tensor(x1_sb[:, oc * 1024:(oc + 1) * 1024],
                                                   pp, xo_b[sl][:], ALU.add), "tV")
                    xfree[sl] = tv
                    evtok[sl] = tv
                    VEC.tensor_copy(xbb[sl][:], x1_sb[:, oc * 1024:(oc + 1) * 1024])
                    tvb = bld.inc(VEC.engine_nop(), "tV")
                    bld.wait(SCA, tv)
                    bld.wait(SCA, stfree[sl])
                    tc = bld.inc(SCA.activation(sqb[sl][:],
                                                x1_sb[:, oc * 1024:(oc + 1) * 1024],
                                                AF.Square), "tC")
                    bld.wait(TEN, tvb)
                    bld.wait(TEN, tc)
                    last = None
                    for tt in range(2):
                        TEN.matmul(psx[0:1, tt * 512:(tt + 1) * 512], ones_b[:],
                                   xbb[sl][:, tt * 512:(tt + 1) * 512],
                                   start=(oc == 0), stop=(oc == HC - 1))
                        last = TEN.matmul(psq[0:1, tt * 512:(tt + 1) * 512], ones_b[:],
                                          sqb[sl][:, tt * 512:(tt + 1) * 512],
                                          start=(oc == 0), stop=(oc == HC - 1))
                    stfree[sl] = bld.inc(last, "tT")
                bld.wait(TEN, evtok[0])
                bld.wait(TEN, evtok[1])
                with ExitStack() as st5x:
                    rowf = st5x.enter_context(nc.sbuf_tensor("rowf", [1, 2 * 1024], F32))
                    bld.wait(VEC, stfree[0])
                    bld.wait(VEC, stfree[1])
                    W = 1024
                    VEC.tensor_scalar(rowf[0:1, 0:W], psx[0:1, :], 1.0 / H, None, ALU.mult)
                    VEC.tensor_scalar(rowf[0:1, W:2 * W], psq[0:1, :], 1.0 / H, None, ALU.mult)
                    VEC.scalar_tensor_tensor(psx[0:1, :], rowf[0:1, 0:W], -1.0,
                                             rowf[0:1, 0:W], ALU.mult, ALU.mult)
                    tv = bld.inc(VEC.tensor_tensor(rowf[0:1, W:2 * W], rowf[0:1, W:2 * W],
                                                   psx[0:1, :], ALU.add), "tV")
                    bld.wait(SCA, tv)
                    SCA.activation(rowf[0:1, W:2 * W], rowf[0:1, W:2 * W], AF.Ln,
                                   bias=eps_ln[:])
                    tc = bld.inc(SCA.activation(rowf[0:1, W:2 * W], rowf[0:1, W:2 * W],
                                                AF.Exp, scale=-0.5), "tC")
                    bld.wait(VEC, tc)
                    VEC.tensor_tensor(psx[0:1, :], rowf[0:1, 0:W], rowf[0:1, W:2 * W],
                                      ALU.mult)
                    tv = bld.inc(VEC.tensor_copy(rowf[0:1, 0:W], psx[0:1, :]), "tV")
                    bld.wait(TEN, tv)
                    p45.close()
                    with ExitStack() as st5y:
                        psb = st5y.enter_context(nc.psum_tensor("fpsb", [128, 2 * 1024], F32))
                        rf_b = st5y.enter_context(nc.sbuf_tensor("rf_b", [128, 1024], F32))
                        mrf_b = st5y.enter_context(nc.sbuf_tensor("mrf_b", [128, 1024], F32))
                        ftmp = st5y.enter_context(nc.sbuf_tensor("ftmp", [128, 1024], F32))
                        W = 1024
                        last = None
                        for tt in range(2):
                            TEN.matmul(psb[:, tt * 512:(tt + 1) * 512], ones_fr[:],
                                       rowf[0:1, W + tt * 512: W + (tt + 1) * 512],
                                       start=True, stop=True)
                            last = TEN.matmul(psb[:, W + tt * 512: W + (tt + 1) * 512],
                                              ones_fr[:], rowf[0:1, tt * 512:(tt + 1) * 512],
                                              start=True, stop=True)
                        tmm = bld.inc(last, "tT")
                        bld.wait(VEC, tmm)
                        VEC.tensor_copy(rf_b[:], psb[:, 0:W])
                        VEC.tensor_copy(mrf_b[:], psb[:, W:2 * W])
                        pre_norm_fence[:] = [bld.tok("tV"), bld.tok("tC"), bld.tok("tT")]
                        xf_tok.clear()
                        for oc in range(HC):
                            VEC.tensor_tensor(ftmp[:], x1_sb[:, oc * 1024:(oc + 1) * 1024],
                                              rf_b[:], ALU.mult)
                            xf_tok.append(bld.inc(
                                VEC.tensor_tensor(xf_sb[:, oc * 1024:(oc + 1) * 1024],
                                                  ftmp[:], mrf_b[:], ALU.subtract), "tV"))
            cbar()
            lvAA.close()
            tap("x1", x1_sb[:], (128, HC * 1024))
            tap("xf", xf_sb[:], (128, HC * 1024))

            bld.wait(GP, pre_norm_fence[0])
            bld.wait(GP, pre_norm_fence[1])
            bld.wait(GP, pre_norm_fence[2])
            bld.wait(SY, pre_norm_fence[0])
            bld.wait(SY, pre_norm_fence[1])
            bld.wait(SY, pre_norm_fence[2])
            bld.wait(SCA, t_const)
            with ExitStack() as f6:
                S6 = f6.enter_context
                h_sb = S6(nc.sbuf_tensor("hh", [128, 32 * 1024], BF16))
                gt_b = [S6(nc.sbuf_tensor(f"gt{i}", [128, HC * 128], BF16)) for i in range(2)]
                ut_b = [S6(nc.sbuf_tensor(f"ut{i}", [128, HC * 128], BF16)) for i in range(2)]
                dt_b = [S6(nc.sbuf_tensor(f"dt{i}", [128, 32 * 128], BF16)) for i in range(2)]
                sg_b = [S6(nc.sbuf_tensor(f"sg{i}", [128, 1024], BF16)) for i in range(2)]
                oc_b = [S6(nc.sbuf_tensor(f"oc{i}", [128, 1024], F32)) for i in range(2)]
                pg = [S6(nc.psum_tensor(f"pg{i}", [128, 1024], F32)) for i in range(2)]
                pu = [S6(nc.psum_tensor(f"pu{i}", [128, 1024], F32)) for i in range(2)]
                gfree = [None, None]
                ufree = [None, None]
                dfree = [None, None]
                ocfree = [None, None]
                hread = None
                for half in range(2):
                    t_hmul = [None, None]
                    t_silu = [None, None]
                    for ib in range(32):
                        ibg = half * 32 + ib
                        sl = ibg % 2
                        r3 = ibg % 2
                        t_g = dma_g(f"gt{r3}", gt_b[r3][:], gwD[ibg], free_tok=gfree[r3])
                        t_u = dma_g(f"ut{r3}", ut_b[r3][:], uwD[ibg], free_tok=ufree[r3])
                        bld.wait(SCA, t_g)
                        SCA.activation(gt_b[r3][:], gt_b[r3][:], AF.Identity, bias=n128[:])
                        bld.wait(SCA, t_u)
                        t_sh = bld.inc(SCA.activation(ut_b[r3][:], ut_b[r3][:], AF.Identity,
                                                      bias=n128[:]), "tC")
                        bld.wait(TEN, t_sh)
                        bld.wait(TEN, t_hmul[sl])
                        last = None
                        for tt in range(2):
                            for hcc in range(HC):
                                if ibg == 0 and xf_tok:
                                    bld.wait(TEN, xf_tok[hcc])
                                last = TEN.matmul(
                                    pg[sl][:, tt * 512:(tt + 1) * 512],
                                    gt_b[r3][:, hcc * 128:(hcc + 1) * 128],
                                    xf_sb[:, hcc * 1024 + tt * 512: hcc * 1024 + (tt + 1) * 512],
                                    start=(hcc == 0), stop=(hcc == HC - 1))
                        t_gmm = bld.inc(last, "tT")
                        gfree[r3] = t_gmm
                        last = None
                        for tt in range(2):
                            for hcc in range(HC):
                                last = TEN.matmul(
                                    pu[sl][:, tt * 512:(tt + 1) * 512],
                                    ut_b[r3][:, hcc * 128:(hcc + 1) * 128],
                                    xf_sb[:, hcc * 1024 + tt * 512: hcc * 1024 + (tt + 1) * 512],
                                    start=(hcc == 0), stop=(hcc == HC - 1))
                        t_umm = bld.inc(last, "tT")
                        ufree[r3] = t_umm
                        bld.wait(SCA, t_gmm)
                        bld.wait(SCA, t_hmul[sl])
                        t_silu[sl] = bld.inc(SCA.activation(sg_b[sl][:], pg[sl][:], AF.Silu,
                                                            scale=gs_sb[:, ibg:ibg + 1]), "tC")
                        bld.wait(VEC, t_silu[sl])
                        bld.wait(VEC, t_umm)
                        bld.wait(VEC, hread)
                        t_hmul[sl] = bld.inc(
                            VEC.scalar_tensor_tensor(h_sb[:, ib * 1024:(ib + 1) * 1024],
                                                     pu[sl][:], us_sb[:, ibg:ibg + 1], sg_b[sl][:],
                                                     ALU.mult, ALU.mult), "tV")
                    t_hall = bld.tok("tV")
                    dev = [None, None]
                    for oc in range(HC):
                        sl = oc % 2
                        t_d = dma_g(f"dt{sl}", dt_b[sl][:],
                                    dwD[oc, :, half * 32:(half + 1) * 32, :], free_tok=dfree[sl])
                        bld.wait(SCA, t_d)
                        t_sh = bld.inc(SCA.activation(dt_b[sl][:], dt_b[sl][:], AF.Identity,
                                                      bias=n128[:]), "tC")
                        pp = pg[sl]
                        bld.wait(TEN, t_sh)
                        bld.wait(TEN, t_hall)
                        bld.wait(TEN, t_hmul[sl])
                        bld.wait(TEN, dev[sl])
                        last = None
                        for tt in range(2):
                            for ib in range(32):
                                last = TEN.matmul(
                                    pp[:, tt * 512:(tt + 1) * 512],
                                    dt_b[sl][:, ib * 128:(ib + 1) * 128],
                                    h_sb[:, ib * 1024 + tt * 512: ib * 1024 + (tt + 1) * 512],
                                    start=(ib == 0), stop=(ib == 31))
                        tmm = bld.inc(last, "tT")
                        dfree[sl] = tmm
                        bld.wait(VEC, tmm)
                        if half == 0:
                            tv = bld.inc(VEC.scalar_tensor_tensor(
                                x1_sb[:, oc * 1024:(oc + 1) * 1024], pp[:],
                                ds_sb[:, oc:oc + 1], x1_sb[:, oc * 1024:(oc + 1) * 1024],
                                ALU.mult, ALU.add), "tV")
                            dev[sl] = tv
                        else:
                            bld.wait(VEC, ocfree[sl])
                            tv = bld.inc(VEC.scalar_tensor_tensor(
                                oc_b[sl][:], pp[:], ds_sb[:, oc:oc + 1],
                                x1_sb[:, oc * 1024:(oc + 1) * 1024],
                                ALU.mult, ALU.add), "tV")
                            dev[sl] = tv
                            ocfree[sl] = dma_s(f"out{sl}", outT[oc * 128:(oc + 1) * 128, :],
                                               oc_b[sl][:], free_tok=tv)
                    hread = bld.tok("tT")
        SY.wait_ge(bld.sem("s_out0"), bld.cnt["s_out0"])
        SY.wait_ge(bld.sem("s_out1"), bld.cnt["s_out1"])
        nc.all_engine_barrier()
    return nc



_PERM_A = np.concatenate([np.arange(0, 64, 2), np.arange(1, 64, 2)])
_PERM_B = np.concatenate([np.arange(1, 64, 2), np.arange(0, 64, 2)])


def _rope_tables():
    j = np.arange(32, dtype=np.float64)
    inv = 10000.0 ** (-(j * 2.0 / 64.0))
    pos = np.arange(T, dtype=np.float64)
    ang = pos[:, None] * inv[None, :]
    return np.cos(ang), np.sin(ang)


def _stage_shared(inp):
    f32 = np.float32
    s = {}
    s["w_cq"] = np.ascontiguousarray(np.asarray(inp["L0_attn_W_cq"], f32))
    s["w_ckv"] = np.ascontiguousarray(np.asarray(inp["L0_attn_W_ckv"], f32))
    s["qsT"] = np.ascontiguousarray(np.asarray(inp["L0_attn_q_scale"], f32).reshape(QRC, 128).T)
    s["kvsT"] = np.ascontiguousarray(np.asarray(inp["L0_attn_kv_scale"], f32).reshape(KVC, 128).T)
    wqn = np.asarray(inp["L0_attn_W_dq_nope"], f32)
    s["wqn"] = np.ascontiguousarray(wqn.reshape(QRC, 128, NH, 128).transpose(2, 1, 0, 3))
    wr = np.asarray(inp["L0_attn_W_dq_rope"], f32).reshape(QR, NH, 64)
    wra = wr[:, :, _PERM_A].reshape(QR, NH // 2, 128)
    wrb = wr[:, :, _PERM_B].reshape(QR, NH // 2, 128)
    s["wqrA"] = np.ascontiguousarray(wra.reshape(QRC, 128, NH // 2, 128).transpose(2, 1, 0, 3))
    s["wqrB"] = np.ascontiguousarray(wrb.reshape(QRC, 128, NH // 2, 128).transpose(2, 1, 0, 3))
    wdk = np.asarray(inp["L0_attn_W_dk_nope"], f32)
    s["wdk"] = np.ascontiguousarray(wdk.reshape(KVC, 128, NH, 128).transpose(2, 1, 0, 3))
    wdv = np.asarray(inp["L0_attn_W_dv"], f32)
    s["wdv"] = np.ascontiguousarray(wdv.reshape(KVC, 128, NH, 128).transpose(2, 1, 0, 3))
    wkr = np.asarray(inp["L0_attn_W_krope"], f32)
    s["wkrA"] = np.ascontiguousarray(wkr[:, _PERM_A].reshape(HC, 128, 64).transpose(1, 0, 2))
    s["wkrB"] = np.ascontiguousarray(wkr[:, _PERM_B].reshape(HC, 128, 64).transpose(1, 0, 2))
    wo = np.asarray(inp["L0_attn_W_o"], f32)
    s["wo"] = np.ascontiguousarray(wo.reshape(NH, 128, HC, 128).transpose(2, 1, 0, 3))
    def _u8(a):
        a = np.ascontiguousarray(np.asarray(a, np.int32))
        return a.view(np.uint8).reshape(a.shape + (4,))[..., 0]
    s["gw"] = np.ascontiguousarray(
        _u8(inp["L0_ffn_gate_w"]).reshape(NIB, 128, HC, 128).transpose(0, 3, 2, 1))
    s["uw"] = np.ascontiguousarray(
        _u8(inp["L0_ffn_up_w"]).reshape(NIB, 128, HC, 128).transpose(0, 3, 2, 1))
    s["dw"] = np.ascontiguousarray(
        _u8(inp["L0_ffn_down_w"]).reshape(HC, 128, NIB, 128).transpose(0, 3, 2, 1))
    s["gsT"] = np.ascontiguousarray(np.asarray(inp["L0_ffn_gate_scale"], f32).reshape(NIB, 128).T)
    s["usT"] = np.ascontiguousarray(np.asarray(inp["L0_ffn_up_scale"], f32).reshape(NIB, 128).T)
    s["dsT"] = np.ascontiguousarray(np.asarray(inp["L0_ffn_down_scale"], f32).reshape(HC, 128).T)
    s["ident"] = np.eye(128, dtype=ml_dtypes.bfloat16)
    cosT, sinT = _rope_tables()
    s["cck"] = np.ascontiguousarray(np.concatenate([cosT.T, cosT.T], 0) / 16.0).astype(f32)
    s["ssk"] = np.ascontiguousarray(np.concatenate([-sinT.T, sinT.T], 0) / 16.0).astype(f32)
    s["_cosT"], s["_sinT"] = cosT, sinT
    return s


def _stage_core(inp, s, core):
    b, side = core // 2, core % 2
    f32 = np.float32
    d = {}
    x = np.asarray(inp["L0_attn_x"], f32)
    xTb = np.ascontiguousarray(x[b].T)
    d["xT"] = xTb
    d["xownT"] = np.ascontiguousarray(xTb.reshape(H, HC, 128)[:, side::2, :].reshape(H, NOWN))
    ownpos = np.arange(NOWN) // 128 * 256 + side * 128 + np.arange(NOWN) % 128
    cq = s["_cosT"].T[:, ownpos]
    sq = s["_sinT"].T[:, ownpos]
    sc = 1.0 / np.sqrt(np.float64(192.0))
    d["ccq"] = np.ascontiguousarray(np.concatenate([cq, cq, cq, cq], 0) * sc).astype(f32)
    d["ssq"] = np.ascontiguousarray(np.concatenate([-sq, sq, -sq, sq], 0) * sc).astype(f32)
    m = np.zeros((128, 256), f32)
    tri = np.where(np.arange(128)[:, None] >= np.arange(128)[None, :], 0.0, NEGBIG).astype(f32)
    if side == 0:
        m[:, 0:128] = tri
        m[:, 128:256] = NEGBIG
    else:
        m[:, 128:256] = tri
    d["smask"] = m
    return d


def kernel(**inp):
    s = _stage_shared(inp)
    nc = build_program()
    shared = {k: v for k, v in s.items() if not k.startswith("_")}
    in_maps = []
    for c in range(8):
        m = dict(shared)
        m.update(_stage_core(inp, s, c))
        in_maps.append(m)
    res = run_bass_kernel_spmd(nc, in_maps, core_ids=list(range(8)))
    out = np.empty((B, T, H), np.float32)
    for c in range(8):
        b, side = c // 2, c % 2
        sh = np.asarray(res.results[c]["outT"])
        v = sh.reshape(H, NSLOT, 128)
        for sl in range(NSLOT):
            t0 = 128 * (2 * sl + side)
            out[b, t0:t0 + 128, :] = v[:, sl, :].T
    return out
